# version 14
# baseline (speedup 1.0000x reference)
import numpy as np
from contextlib import ExitStack

import concourse.bass as bass
import concourse.mybir as mybir
from concourse.bass_utils import run_bass_kernel_spmd
from concourse.bass_core_rust import IndirectOffsetOnAxis

F32 = mybir.dt.float32
BF16 = mybir.dt.bfloat16
I32 = mybir.dt.int32
AF = mybir.ActivationFunctionType
OP = mybir.AluOpType

S = 2048
D = 1024
H = 16
HD = 64
NSEQ = 2
TT = S // 128
NE = 8
KSEL = 512
FF = 4096
NROW = NSEQ * S

ENGS = ("sy", "te", "sc", "ve", "gp")


class Ctx:

    def __init__(self, nc, sems, engines, cur, first):
        self.nc = nc
        self.sems = sems
        self.eng = engines
        self.cur = cur
        self.first = first
        self.cnt = {k: 0 for k in sems}
        self._lastw = {}

    def w(self, eng, sem, val=None):
        if val is None:
            val = self.cnt[sem]
        if val <= 0:
            return
        key = (eng, sem)
        if self._lastw.get(key, -1) >= val:
            return
        self._lastw[key] = val
        if self.cur == eng:
            self.eng[eng].wait_ge(self.sems[sem], val)

    def do(self, eng, fn, inc=None, inc_n=1, flush=False):
        if flush:
            fs = {"ve": "fve", "sc": "fsc"}[eng]
            self.cnt[fs] += 1
        if self.cur == eng:
            ins = fn(self.eng[eng])
            if flush:
                ins.then_inc(self.sems[fs], 1)
                self.eng[eng].wait_ge(self.sems[fs], self.cnt[fs])
                if inc is not None:
                    self.eng[eng].sem_inc(self.sems[inc], inc_n)
            elif inc is not None:
                ins.then_inc(self.sems[inc], inc_n)
        if inc is not None:
            self.cnt[inc] += inc_n
        return self.cnt[inc] if inc is not None else None


def build_nc(debug=(), stop_after=None, dbg_store=None):
    _ORDER = ["INIT", "A0", "B0", "C0", "A1", "B1", "C1", "D", "E"]
    _inc = set(_ORDER if stop_after is None else _ORDER[:_ORDER.index(stop_after) + 1])
    nc = bass.Bass("TRN2", target_bir_lowering=False, detect_race_conditions=False)

    dx = nc.dram_tensor("x", [NSEQ, S, D], F32, kind="ExternalInput").ap()
    dxt = nc.dram_tensor("xt", [NSEQ, D, S], F32, kind="ExternalInput").ap()
    dWq = nc.dram_tensor("Wq", [D, D], F32, kind="ExternalInput").ap()
    dWk = nc.dram_tensor("Wk", [D, D], F32, kind="ExternalInput").ap()
    dWv = nc.dram_tensor("Wv", [D, D], F32, kind="ExternalInput").ap()
    dWo = nc.dram_tensor("Wo", [D, D], F32, kind="ExternalInput").ap()
    dbqT = nc.dram_tensor("bqT", [128, 8], F32, kind="ExternalInput").ap()
    dbkT = nc.dram_tensor("bkT", [128, 8], F32, kind="ExternalInput").ap()
    dbv = nc.dram_tensor("bv", [1, D], F32, kind="ExternalInput").ap()
    dbo = nc.dram_tensor("bo", [1, D], F32, kind="ExternalInput").ap()
    dgm = nc.dram_tensor("gamma", [1, D], F32, kind="ExternalInput").ap()
    dbt = nc.dram_tensor("beta", [1, D], F32, kind="ExternalInput").ap()
    dgate = nc.dram_tensor("gate_w", [D, NE], F32, kind="ExternalInput").ap()
    dw1 = nc.dram_tensor("w1", [NE, D, FF], F32, kind="ExternalInput").ap()
    d_w1b = nc.dram_tensor("w1b", [NE, 128, 8 * 8 * 512], BF16, kind="Internal").ap()
    dw2 = nc.dram_tensor("w2", [NE, FF, D], F32, kind="ExternalInput").ap()
    dident = nc.dram_tensor("ident", [128, 128], F32, kind="ExternalInput").ap()
    dne4 = nc.dram_tensor("ne4", [40, 1], F32, kind="ExternalInput").ap()

    dout = nc.dram_tensor("out", [NSEQ, S, D], F32, kind="ExternalOutput").ap()

    def scratch(name, shape, dtype):
        kind = "ExternalOutput" if name in debug else "Internal"
        return nc.dram_tensor(name, shape, dtype, kind=kind).ap()

    dbg_taps = {}
    for nm, shp, dt_ in (("dbg_z", [128, D], F32), ("dbg_u", [128, D], F32),
                         ("dbg_x1t", [128, D], F32), ("dbg_stats", [128, 16], F32),
                         ("dbg_oT", [128, 8 * 512], BF16),
                         ("dbg_qT", [128, 8 * 512], BF16),
                         ("dbg_kT", [128, 8 * S], BF16),
                         ("dbg_vaug", [128, TT * H * 65], BF16)):
        if nm in debug:
            dbg_taps[nm] = nc.dram_tensor(nm, shp, dt_, kind="ExternalOutput").ap()

    d_x1f = scratch("x1f", [NROW, D], F32)
    d_x1b = scratch("x1b", [NROW, D], BF16)
    d_acc = scratch("accb", [NROW, D], BF16)
    d_mt = scratch("mt", [NE * 2 * KSEL, 2], F32)

    SB0 = ((nc.sbuf_base + 31) // 32) * 32
    SBTOP = nc.sbuf_top - 32
    off = [SB0]
    offsets = {}

    def at(name, shape, dtype, where=None):
        nbytes = int(np.prod(shape[1:])) * (2 if dtype == BF16 else 4)
        if where is None:
            o = off[0]
            off[0] = o + ((nbytes + 31) // 32) * 32
        else:
            o = where
        offsets[name] = o
        assert o + nbytes <= SBTOP, f"sbuf overflow {name}: {o}+{nbytes}"
        return nc.alloc_sbuf_tensor_at(name, shape, dtype, offset=o).ap(), o + nbytes

    ident_f, _ = at("ident_f", [128, 128], F32)
    ident_b, _ = at("ident_b", [128, 128], BF16)
    gate_b, _ = at("gate_b", [128, NE * 8], BF16)
    bqT, _ = at("bqT_s", [128, 8], F32)
    bkT, _ = at("bkT_s", [128, 8], F32)
    bv_row, _ = at("bv_row", [1, D], BF16)
    bo_row, _ = at("bo_row", [1, D], BF16)
    ones_r, _ = at("ones_r", [1, 128], BF16)
    ones_rf, _ = at("ones_rf", [1, 128], F32)
    gm_bc, _ = at("gm_bc", [128, D], F32)
    bt_bc, _ = at("bt_bc", [128, D], F32)
    zero_b, _ = at("zero_b", [128, D], BF16)
    probs_tok, _ = at("probs_tok", [128, NSEQ * TT * 8], F32)
    probs_en, _ = at("probs_en", [40, S], F32)
    ne4_sb, _ = at("ne4_sb", [40, 1], F32)
    stats, _ = at("stats", [128, 24], F32)
    small, _ = at("small", [40, 16], F32)
    smalli, _ = at("smalli", [40, 8], I32)
    m_sb, _ = at("m_sb", [128, 128], F32)
    gsrc, _ = at("gsrc", [128, 16], F32)
    gsrci, _ = at("gsrci", [128, 8], I32)
    m_i32, _ = at("m_i32", [128, 64], I32)
    PERS_END = off[0]

    wsl0, _ = at("wsl0", [128, 8 * D], BF16)
    wsl1, _ = at("wsl1", [128, 8 * D], BF16)
    A_W_END = off[0]
    kT, _ = at("kT", [128, 8 * S], BF16)
    vaug, _ = at("vaug", [128, TT * H * 65], BF16)
    A_KV_END = off[0]
    qTq, _ = at("qTq", [128, 8 * 512], BF16)
    oT, _ = at("oT", [128, 8 * 512], BF16)
    pTT, _ = at("pTT", [128, 2048], BF16)
    pT = [pTT[:, 0:512], pTT[:, 512:1024], pTT[:, 1024:1536], pTT[:, 1536:2048]]
    o_full = []
    for i in range(2):
        t, _ = at(f"o_full{i}", [128, 1024], F32)
        o_full.append(t)
    epi_z, _ = at("epi_z", [128, D], F32)
    epi_t1, _ = at("epi_t1", [128, D], F32)
    A_B_ONLY_END = off[0]
    xTq = []
    for i in range(2):
        t, _ = at(f"xTq{i}", [128, 8 * 512], BF16)
        xTq.append(t)
    xe_f = []
    for i in range(2):
        t, _ = at(f"xe_f{i}", [128, D], F32)
        xe_f.append(t)
    x1f_st = []
    x1b_st = []
    for i in range(2):
        t, _ = at(f"x1f_st{i}", [128, D], F32)
        x1f_st.append(t)
        t, _ = at(f"x1b_st{i}", [128, D], BF16)
        x1b_st.append(t)
    if dbg_store is not None:
        t, _ = at("x1f_st2", [128, D], F32)
        x1f_st.append(t)
    x1T_t, _ = at("x1T_t", [128, 8 * 128], BF16)
    A_END = off[0]

    offC = [None]

    def atC(name, shape, dtype):
        nbytes = int(np.prod(shape[1:])) * (2 if dtype == BF16 else 4)
        o = offC[0]
        offC[0] = o + ((nbytes + 31) // 32) * 32
        offsets[name] = o
        assert offC[0] <= SBTOP, f"sbuf overflow {name}"
        return nc.alloc_sbuf_tensor_at(name, shape, dtype, offset=o).ap()


    offF = [PERS_END]

    def atF(name, shape, dtype):
        nbytes = int(np.prod(shape[1:])) * (2 if dtype == BF16 else 4)
        o = offF[0]
        offF[0] = o + ((nbytes + 31) // 32) * 32
        offsets[name] = o
        assert offF[0] <= SBTOP, f"sbuf overflow {name}"
        return nc.alloc_sbuf_tensor_at(name, shape, dtype, offset=o).ap()

    hT = atF("hT", [128, 32 * 512], BF16)
    w2b = atF("w2b", [128, 32 * D], BF16)
    xinT = atF("xinT", [128, 8 * 512], BF16)
    xin_rows = atF("xin_rows", [128, D], BF16)
    y_w = []
    for i in range(2):
        y_w.append(atF(f"y_w{i}", [128, D], BF16))
    w1g = [atF("w1g0", [128, 8 * 512], BF16),
           atF("w1g1", [128, 8 * 512], BF16)]
    gm_row = atF("gm_row", [1, D], F32)
    bt_row = atF("bt_row", [1, D], F32)


    F_END = offF[0]
    assert F_END <= SBTOP

    offE = [F_END]

    def atE(name, shape, dtype):
        nbytes = int(np.prod(shape[1:])) * (2 if dtype == BF16 else 4)
        o = offE[0]
        offE[0] = o + ((nbytes + 31) // 32) * 32
        assert offE[0] <= SBTOP, f"sbuf overflow {name}"
        return nc.alloc_sbuf_tensor_at(name, shape, dtype, offset=o).ap()

    e_x1 = [atE("e_x1_0", [128, D], F32), atE("e_x1_1", [128, D], F32)]
    e_ac = [atE("e_ac_0", [128, D], BF16), atE("e_ac_1", [128, D], BF16)]
    e_af = [atE("e_af0", [128, D], F32), atE("e_af1", [128, D], F32)]
    e_u = [atE("e_u0", [128, D], F32), atE("e_u1", [128, D], F32)]
    e_o = [atE("e_o_0", [128, D], F32), atE("e_o_1", [128, D], F32)]

    def alloc_cset(tag, base):
        o = [base]

        def a(name, shape, dtype):
            nb = int(np.prod(shape[1:])) * (2 if dtype == BF16 else 4)
            off_ = o[0]
            o[0] = off_ + ((nb + 31) // 32) * 32
            assert o[0] <= SBTOP, f"sbuf overflow {name}{tag}"
            return nc.alloc_sbuf_tensor_at(f"{name}{tag}", shape, dtype, offset=off_).ap()

        cs = {
            "dummy": a("c_dummy", [40, S], BF16),
            "A": a("c_A", [40, S], BF16),
            "T": a("c_T", [40, S], BF16),
            "csS": a("c_csS", [40, 512], F32),
            "csT": a("c_csT", [40, 512], F32),
            "r": a("c_r", [40, 512], F32),
            "j": a("c_j", [40, 512], F32),
            "k1": a("c_k1", [40, 512], BF16),
            "keep": a("c_keep", [40, 512], I32),
            "c255": a("c_255", [40, 512], F32),
            "q_pl": a("c_q_pl", [40, S], BF16),
            "mj_pl": a("c_mj_pl", [40, S], BF16),
            "ones": a("c_ones", [40, 512], F32),
            "f32": a("c_f32", [40, 512], F32),
            "qf": a("c_qf", [128, 40], F32),
            "of": a("c_of", [128, 40], F32),
            "oi": a("c_oi", [128, 40], I32),
        }
        cs["_end"] = o[0]
        return cs

    CS = [alloc_cset("_s0", offsets["qTq"]), alloc_cset("_s1", F_END)]
    xin4_off = ((max(A_END, CS[1]["_end"], offE[0]) + 31) // 32) * 32
    assert xin4_off + 4 * D * 2 <= SBTOP, "xin4 overflow"
    xin4 = nc.alloc_sbuf_tensor_at("xin4", [128, 4 * D], BF16, offset=xin4_off).ap()




    es = ExitStack()
    SD0 = es.enter_context(nc.psum_tensor("sd0", [128, 1024], F32))
    SD1 = es.enter_context(nc.psum_tensor("sd1", [128, 1024], F32))
    B47 = [es.enter_context(nc.psum_tensor(f"bank{i}", [128, 512], F32)) for i in range(4, 8)]
    B = [SD0.ap()[:, 0:512], SD0.ap()[:, 512:1024],
         SD1.ap()[:, 0:512], SD1.ap()[:, 512:1024],
         B47[0].ap(), B47[1].ap(), B47[2].ap(), B47[3].ap()]
    SDP = [SD0.ap(), SD1.ap()]

    sem_names = [
        "dXa", "dXb",
        "dW",
        "dX1",
        "dX1b",
        "dMt",
        "dAcc",
        "dW1a", "dW1b",
        "dWc",
        "dW2",
        "dGa",
        "dMs",
        "dScA", "dScB",
        "dEx",
        "dTap",
        "dOut",
        "pe", "sc", "ve", "gp",
        "fve", "fsc",
    ]
    sems = {}
    for s_ in sem_names:
        sems[s_] = es.enter_context(nc.semaphore(s_))

    engines = {"sy": nc.sync, "te": nc.tensor, "sc": nc.scalar,
               "ve": nc.vector, "gp": nc.gpsimd}

    def walk(c: Ctx):
        marks = {}

        def mark(key, sem=None, val=None):
            if sem is None:
                marks.pop(key, None)
            else:
                marks[key] = (sem, val if val is not None else c.cnt[sem])

        def wait_mark(eng, key):
            if key in marks:
                sem, val = marks[key]
                c.w(eng, sem, val)

        ALL_SEMS = ["dXa", "dXb", "dW", "dX1", "dX1b", "dMt", "dAcc", "dW1a", "dW1b", "dWc",
                    "dW2", "dGa", "dMs", "dScA", "dScB", "dEx", "dOut", "dTap",
                    "fve", "fsc",
                    "pe", "sc", "ve", "gp"]

        def tap(nm, src_ap, eng_sem, val):
            if nm in dbg_taps:
                c.w("gp", eng_sem, val)
                c.do("gp", lambda e, nm=nm, src_ap=src_ap: e.dma_start(
                    out=dbg_taps[nm], in_=src_ap), inc="dTap", inc_n=16)

        bnd = {}
        if c.cur == "gp":
            for nm, v in (("mt", NE * 2 * KSEL - 1), ("acc", NROW - 1)):
                r = c.eng["gp"].alloc_register(f"bnd_{nm}")
                c.eng["gp"].reg_mov(r, v)
                bnd[nm] = r

        def phase_gate():
            for eng in ENGS:
                for sm in ALL_SEMS:
                    c.w(eng, sm, c.cnt[sm])

        c.do("gp", lambda e: e.dma_start(out=ident_f[:], in_=dident), inc="dW", inc_n=16)
        c.do("gp", lambda e: e.dma_start(out=ident_b[:], in_=dident), inc="dW", inc_n=16)
        c.do("gp", lambda e: e.dma_start(
            out=gate_b[:].rearrange("p (c w) -> p c w", w=8),
            in_=dgate.rearrange("(c p) w -> p c w", p=128)), inc="dW", inc_n=16)
        c.do("gp", lambda e: e.dma_start(out=bqT[:], in_=dbqT), inc="dW", inc_n=16)
        c.do("gp", lambda e: e.dma_start(out=bkT[:], in_=dbkT), inc="dW", inc_n=16)
        c.do("gp", lambda e: e.dma_start(out=bv_row[:], in_=dbv), inc="dW", inc_n=16)
        c.do("gp", lambda e: e.dma_start(out=bo_row[:], in_=dbo), inc="dW", inc_n=16)
        c.do("gp", lambda e: e.dma_start(out=gm_row[:], in_=dgm), inc="dW", inc_n=16)
        c.do("gp", lambda e: e.dma_start(out=bt_row[:], in_=dbt), inc="dW", inc_n=16)
        c.do("gp", lambda e: e.dma_start(out=ne4_sb[:], in_=dne4), inc="dW", inc_n=16)
        init_w = c.cnt["dW"]

        c.w("ve", "dW", init_w)
        c.do("ve", lambda e: e.memset(ones_r[:], 1.0))
        c.do("ve", lambda e: e.memset(ones_rf[:], 1.0))
        c.do("ve", lambda e: e.memset(zero_b[:], 0.0))
        c.do("ve", lambda e: e.memset(probs_en[:], 0.0), inc="ve")
        ve_init = c.cnt["ve"]

        for half in range(2):
            c.w("te", "dW", init_w)
            c.w("te", "ve", ve_init)
            if half == 1:
                c.w("te", "sc", c.cnt["sc"])
            c.do("te", lambda e, h=half: e.matmul(
                B[6][:, 0:512], ones_rf[:], gm_row[:, h * 512:(h + 1) * 512],
                start=True, stop=True), inc="pe")
            c.do("te", lambda e, h=half: e.matmul(
                B[7][:, 0:512], ones_rf[:], bt_row[:, h * 512:(h + 1) * 512],
                start=True, stop=True), inc="pe")
            c.w("sc", "pe", c.cnt["pe"])
            c.do("sc", lambda e, h=half: e.copy(gm_bc[:, h * 512:(h + 1) * 512], B[6][:, 0:512]))
            c.do("sc", lambda e, h=half: e.copy(bt_bc[:, h * 512:(h + 1) * 512], B[7][:, 0:512]),
                 inc="sc")
        bc_done = c.cnt["sc"]
        mark("bank6", "sc", bc_done)
        mark("bank7", "sc", bc_done)

        c.w("gp", "ve", ve_init)
        for r in range(NROW // 128):
            c.do("gp", lambda e, r=r: e.dma_start(
                out=d_acc[r * 128:(r + 1) * 128, :], in_=zero_b[:]), inc="dAcc", inc_n=16)
        c.do("gp", lambda e: e.dma_start(
            out=d_mt.rearrange("(q k) c -> q (k c)", q=128),
            in_=zero_b[:, 0:128]), inc="dMt", inc_n=16)
        mt_prefill = c.cnt["dMt"]

        def gen_stage_c(n):
            cs = CS[n]
            base_p = 0 if n == 0 else 32
            R = probs_en[base_p:base_p + 8, :]
            Rfull = probs_en
            c.w("ve", "sc", c.cnt["sc"])
            c.w("ve", "pe", c.cnt["pe"])
            c.do("ve", lambda e: e.memset(cs["c255"][:], 255.0))
            c.do("ve", lambda e: e.memset(small[:, 0:1], 0.0))
            c.do("ve", lambda e: e.memset(small[:, 1:2], 1.0), inc="ve", flush=True)
            lo, hi = 0, 1
            for it in range(48):
                c.do("ve", lambda e, lo=lo, hi=hi: e.tensor_scalar(
                    small[:, 2:3], small[:, lo:lo + 1], small[:, hi:hi + 1],
                    0.5, OP.add, OP.mult), flush=True)
                c.do("ve", lambda e: e.tensor_scalar(
                    cs["dummy"][:], Rfull[:], small[:, 2:3], None, OP.is_ge, OP.add,
                    accum_out=small[:, 3:4]), flush=True)
                c.do("ve", lambda e: e.tensor_scalar(
                    smalli[:, 0:1], small[:, 3:4], 511.5, None, OP.is_ge), flush=True)
                nlo, nhi = (4, 5) if lo == 0 else (0, 1)
                c.do("ve", lambda e, lo=lo, nlo=nlo: e.select(
                    small[:, nlo:nlo + 1], smalli[:, 0:1], small[:, 2:3],
                    small[:, lo:lo + 1]))
                c.do("ve", lambda e, hi=hi, nhi=nhi: e.select(
                    small[:, nhi:nhi + 1], smalli[:, 0:1], small[:, hi:hi + 1],
                    small[:, 2:3]), inc="ve", flush=True)
                lo, hi = nlo, nhi
                yield
            c.do("ve", lambda e, hi=hi: e.tensor_scalar(
                cs["dummy"][:], Rfull[:], small[:, hi:hi + 1], None, OP.is_ge, OP.add,
                accum_out=small[:, 6:7]), flush=True)
            c.do("ve", lambda e: e.tensor_scalar(
                small[:, 7:8], small[:, 6:7], -1.0, None, OP.add), flush=True)
            c.do("ve", lambda e, hi=hi: e.tensor_scalar(
                cs["A"][:], Rfull[:], small[:, hi:hi + 1], None, OP.is_ge))
            c.do("ve", lambda e, lo=lo: e.tensor_scalar(
                cs["T"][:], Rfull[:], small[:, lo:lo + 1], None, OP.is_ge))
            c.do("ve", lambda e: e.tensor_sub(cs["T"][:], cs["T"][:], cs["A"][:]), inc="ve")
            c.do("ve", lambda e: e.memset(cs["ones"][:], 1.0))
            c.do("ve", lambda e: e.memset(small[:, 8:9], 0.0))
            c.do("ve", lambda e: e.memset(small[:, 9:10], 0.0), inc="ve")
            for ch in range(4):
                sl = slice(ch * 512, (ch + 1) * 512)
                c.do("ve", lambda e, sl=sl: e.tensor_scalar(
                    cs["f32"][:], cs["A"][:, sl], 0.0, None, OP.add))
                c.do("ve", lambda e: e.tensor_tensor_scan(
                    cs["csS"][:], cs["f32"][:], cs["ones"][:], small[:, 8:9], OP.add, OP.mult))
                c.do("ve", lambda e, sl=sl: e.tensor_scalar(
                    cs["f32"][:], cs["T"][:, sl], 0.0, None, OP.add))
                c.do("ve", lambda e: e.tensor_tensor_scan(
                    cs["csT"][:], cs["f32"][:], cs["ones"][:], small[:, 9:10], OP.add, OP.mult))
                c.do("ve", lambda e: e.tensor_copy(small[:, 8:9], cs["csS"][:, 511:512]))
                c.do("ve", lambda e: e.tensor_copy(small[:, 9:10], cs["csT"][:, 511:512]))
                c.do("ve", lambda e: e.tensor_scalar(
                    cs["csS"][:], cs["csS"][:], -1.0, None, OP.add))
                c.do("ve", lambda e: e.tensor_scalar(
                    cs["csT"][:], cs["csT"][:], small[:, 7:8], None, OP.add))
                c.do("ve", lambda e, sl=sl: e.tensor_scalar(
                    cs["keep"][:], cs["A"][:, sl], 0.5, None, OP.is_ge))
                c.do("ve", lambda e: e.select(cs["r"][:], cs["keep"][:], cs["csS"][:], cs["csT"][:]))
                c.do("ve", lambda e, sl=sl: e.tensor_add(cs["k1"][:], cs["A"][:, sl], cs["T"][:, sl]))
                c.do("ve", lambda e: e.tensor_scalar(
                    cs["csT"][:], cs["r"][:], 511.5, None, OP.is_le))
                c.do("ve", lambda e: e.tensor_scalar(
                    cs["csS"][:], cs["k1"][:], 0.0, None, OP.add))
                c.do("ve", lambda e: e.tensor_mul(cs["csS"][:], cs["csS"][:], cs["csT"][:]))
                c.do("ve", lambda e: e.tensor_scalar(
                    cs["keep"][:], cs["csS"][:], 0.5, None, OP.is_ge))
                c.do("ve", lambda e: e.tensor_scalar(cs["j"][:], cs["r"][:], 128.0, None, OP.is_ge))
                c.do("ve", lambda e: e.tensor_scalar(cs["csT"][:], cs["r"][:], 256.0, None, OP.is_ge))
                c.do("ve", lambda e: e.tensor_add(cs["j"][:], cs["j"][:], cs["csT"][:]))
                c.do("ve", lambda e: e.tensor_scalar(cs["csT"][:], cs["r"][:], 384.0, None, OP.is_ge))
                c.do("ve", lambda e: e.tensor_add(cs["j"][:], cs["j"][:], cs["csT"][:]))
                c.do("ve", lambda e: e.scalar_tensor_tensor(
                    cs["csT"][:], cs["j"][:], -128.0, cs["r"][:], OP.mult, OP.add))
                c.do("ve", lambda e, sl=sl: e.select(
                    cs["q_pl"][:, sl], cs["keep"][:], cs["csT"][:], cs["c255"][:]))
                c.do("ve", lambda e, sl=sl: e.tensor_scalar(
                    cs["mj_pl"][:, sl], cs["j"][:], ne4_sb[:], None, OP.add), inc="ve")
                yield
            plane_m = c.cnt["ve"]

            for tt in range(TT):
                tok0 = tt * 128
                c.w("te", "ve", plane_m)
                wait_mark("te", "bank7")
                c.do("te", lambda e, tok0=tok0: e.matmul(
                    B[7][0:128, 0:40], cs["q_pl"][0:40, tok0:tok0 + 128],
                    ident_b[0:40, 0:40], start=True, stop=True))
                c.do("te", lambda e, tok0=tok0: e.matmul(
                    B[7][0:128, 64:104], cs["mj_pl"][0:40, tok0:tok0 + 128],
                    ident_b[0:40, 0:40], start=True, stop=True), inc="pe")
                trp_m = c.cnt["pe"]
                c.w("sc", "pe", trp_m)
                wait_mark("sc", "c_qf")
                c.do("sc", lambda e: e.copy(cs["qf"][:, 0:40], B[7][:, 0:40]), inc="sc")
                qf_m = c.cnt["sc"]
                c.w("ve", "sc", qf_m)
                c.w("ve", "pe", trp_m)
                wait_mark("ve", "c_oi")
                c.do("ve", lambda e: e.scalar_tensor_tensor(
                    cs["of"][:, 0:40], cs["qf"][:, 0:40], 64.0, B[7][:, 64:104],
                    OP.mult, OP.add), flush=True)
                c.do("ve", lambda e: e.tensor_scalar(
                    cs["oi"][:, 0:40], cs["of"][:, 0:40], 0.0, None, OP.add), inc="ve")
                oi_m = c.cnt["ve"]
                mark("bank7", "ve", oi_m)
                mark("c_qf", "ve", oi_m)

                c.w("gp", "ve", ve_init)
                wait_mark("gp", "gsrci")
                c.do("gp", lambda e, n=n, tok0=tok0: e.iota(
                    gsrci[:], pattern=[[0, 8]], base=n * S + tok0,
                    channel_multiplier=1), inc="gp")
                io_m = c.cnt["gp"]
                c.w("ve", "gp", io_m)
                wait_mark("ve", "gsrc")
                c.do("ve", lambda e: e.tensor_scalar(
                    gsrc[:].rearrange("p (k c) -> p k c", c=2)[:, :, 0:1],
                    gsrci[:], 0.0, None, OP.add))
                ptcol = (n * TT + tt) * 8
                c.do("ve", lambda e, ptcol=ptcol: e.tensor_copy(
                    gsrc[:].rearrange("p (k c) -> p k c", c=2)[:, :, 1:2],
                    probs_tok[:, ptcol:ptcol + 8]), inc="ve")
                gsrc_m = c.cnt["ve"]
                mark("gsrci", "ve", gsrc_m)

                c.w("gp", "ve", gsrc_m)
                c.w("gp", "ve", oi_m)
                c.w("gp", "dMt", mt_prefill)
                for e_ in range(8):
                    colx = n * 32 + e_
                    c.do("gp", lambda e, e_=e_, colx=colx: e.indirect_dma_start(
                        out=d_mt,
                        out_offset=IndirectOffsetOnAxis(cs["oi"][:, colx:colx + 1], 0),
                        in_=gsrc[:, 2 * e_:2 * e_ + 2], in_offset=None,
                        bounds_check=bnd["mt"], oob_is_err=False),
                        inc="dMt", inc_n=16)
                mark("gsrc", "dMt", c.cnt["dMt"])
                mark("c_oi", "dMt", c.cnt["dMt"])
                yield

        pending_c = {}
        c_done = {}
        hoist_state = {}
        wc = {}

        def drive_c(n, k):
            g = pending_c.get(n)
            if g is None or c_done.get(n):
                return
            for _ in range(k):
                try:
                    next(g)
                except StopIteration:
                    c_done[n] = True
                    return

        def drain_c(n):
            g = pending_c.get(n)
            if g is None or c_done.get(n):
                return
            while True:
                try:
                    next(g)
                except StopIteration:
                    c_done[n] = True
                    return

        for s in range(NSEQ):
            phase_gate()
            wait_mark("gp", "wsl")
            for slot, dW_ in ((wsl0, dWk), (wsl1, dWv)):
                c.do("gp", lambda e, sl=slot, dd=dW_: e.dma_start(
                    out=sl[:].rearrange("p (c w) -> p c w", w=D),
                    in_=dd.rearrange("(c p) w -> p c w", p=128)), inc="dW", inc_n=16)
            wkv_loaded = c.cnt["dW"]

            if f"A{s}" not in _inc:
                break
            c.do("ve", lambda e: e.memset(
                vaug[:].rearrange("p (t h w) -> p t h w", h=H, w=65)[:, :, :, 64:65],
                1.0), inc="ve")
            vaug_ones = c.cnt["ve"]
            for q4 in range(4):
                xq = xTq[q4 % 2]
                wait_mark("gp", f"xTq{q4 % 2}")
                c.do("gp", lambda e, s=s, q4=q4, xq=xq: e.dma_start(
                    out=xq[:].rearrange("p (k w) -> p k w", w=512),
                    in_=dxt[s].rearrange("(k p) t -> p k t", p=128)
                    [:, :, q4 * 512:(q4 + 1) * 512]), inc="dXa", inc_n=16)
                xq_ld = c.cnt["dXa"]
                c.w("te", "dXa", xq_ld)
                c.w("te", "dW", wkv_loaded)
                for cc in range(8):
                    bk = B[2 + (cc % 2)]
                    wait_mark("te", f"bank{2 + (cc % 2)}")
                    for dinc in range(8):
                        c.do("te", lambda e, cc=cc, dinc=dinc, bk=bk, xq=xq: e.matmul(
                            bk[:],
                            wsl0[:, dinc * D + cc * 128: dinc * D + cc * 128 + 128],
                            xq[:, dinc * 512: dinc * 512 + 512],
                            start=(dinc == 0), stop=(dinc == 7)),
                            inc="pe" if dinc == 7 else None)
                    kp_m = c.cnt["pe"]
                    c.w("sc", "pe", kp_m)
                    c.do("sc", lambda e, cc=cc, bk=bk, q4=q4: e.activation(
                        kT[:, cc * S + q4 * 512: cc * S + q4 * 512 + 512],
                        bk[:], AF.Identity, bias=bkT[:, cc:cc + 1]), inc="sc")
                    mark(f"bank{2 + (cc % 2)}", "sc", c.cnt["sc"])
                for j in range(4):
                    tt = q4 * 4 + j
                    c.w("te", "ve", ve_init)
                    for dh in range(2):
                        bk = B[4 + dh]
                        wait_mark("te", f"bank{4 + dh}")
                        for dinc in range(8):
                            c.do("te", lambda e, dh=dh, dinc=dinc, bk=bk, xq=xq, j=j: e.matmul(
                                bk[:],
                                xq[:, dinc * 512 + j * 128: dinc * 512 + j * 128 + 128],
                                wsl1[:, dinc * D + dh * 512: dinc * D + dh * 512 + 512],
                                start=(dinc == 0), stop=False))
                        c.do("te", lambda e, dh=dh, bk=bk: e.matmul(
                            bk[:], ones_r[:], bv_row[:, dh * 512:(dh + 1) * 512],
                            start=False, stop=True),
                            inc="pe" if dh == 1 else None)
                    vproj_mark = c.cnt["pe"]
                    c.w("sc", "pe", vproj_mark)
                    for dh in range(2):
                        bk = B[4 + dh]
                        base = tt * (H * 65) + dh * 8 * 65
                        c.do("sc", lambda e, bk=bk, base=base: e.copy(
                            vaug[:, base:base + 8 * 65].rearrange(
                                "p (h w) -> p h w", w=65)[:, :, 0:64],
                            bk[:].rearrange("p (h w) -> p h w", w=64)),
                            inc="sc")
                        mark(f"bank{4 + dh}", "sc", c.cnt["sc"])
                mark(f"xTq{q4 % 2}", "pe", c.cnt["pe"])
                if s == 1:
                    drive_c(0, 18)
            if s == 1:
                drain_c(0)
                for eng in ("te", "sc", "ve", "gp", "sy"):
                    c.w(eng, "dMt", c.cnt["dMt"])
                    c.w(eng, "ve", c.cnt["ve"])
                    c.w(eng, "pe", c.cnt["pe"])

            a_done_pe = c.cnt["pe"]
            a_done_sc = c.cnt["sc"]
            if s == 0:
                tap("dbg_kT", kT[:], "sc", a_done_sc)
                tap("dbg_vaug", vaug[:], "sc", a_done_sc)

            c.w("gp", "pe", a_done_pe)
            for slot, dW_ in ((wsl0, dWq), (wsl1, dWo)):
                c.do("gp", lambda e, sl=slot, dd=dW_: e.dma_start(
                    out=sl[:].rearrange("p (c w) -> p c w", w=D),
                    in_=dd.rearrange("(c p) w -> p c w", p=128)), inc="dW", inc_n=16)
            wqo_loaded = c.cnt["dW"]

            if f"B{s}" not in _inc:
                break
            c.w("te", "ve", vaug_ones)
            for qt in range(4):
                xq = xTq[qt % 2]
                wait_mark("gp", f"xTq{qt % 2}")
                c.do("gp", lambda e, s=s, qt=qt, xq=xq: e.dma_start(
                    out=xq[:].rearrange("p (k w) -> p k w", w=512),
                    in_=dxt[s].rearrange("(k p) t -> p k t", p=128)
                    [:, :, qt * 512:(qt + 1) * 512]), inc="dXa", inc_n=16)
                xq_ld = c.cnt["dXa"]
                c.w("te", "dXa", xq_ld)
                c.w("te", "dW", wqo_loaded)
                for cc in range(8):
                    bk = B[2 + (cc % 2)]
                    wait_mark("te", f"bank{2 + (cc % 2)}")
                    for dinc in range(8):
                        c.do("te", lambda e, cc=cc, dinc=dinc, bk=bk, xq=xq: e.matmul(
                            bk[:],
                            wsl0[:, dinc * D + cc * 128: dinc * D + cc * 128 + 128],
                            xq[:, dinc * 512: dinc * 512 + 512],
                            start=(dinc == 0), stop=(dinc == 7)),
                            inc="pe" if dinc == 7 else None)
                    qp_m = c.cnt["pe"]
                    c.w("sc", "pe", qp_m)
                    c.do("sc", lambda e, cc=cc, bk=bk: e.activation(
                        qTq[:, cc * 512: cc * 512 + 512],
                        bk[:], AF.Identity, bias=bqT[:, cc:cc + 1]), inc="sc")
                    mark(f"bank{2 + (cc % 2)}", "sc", c.cnt["sc"])
                mark(f"xTq{qt % 2}", "pe", c.cnt["pe"])
                qT_mark = c.cnt["sc"]
                if s == 0 and qt == 0:
                    tap("dbg_qT", qTq[:], "sc", qT_mark)

                def emit_scores(hp, kc):
                    cchunk = hp
                    pair = 0 if kc % 2 == 0 else 1
                    sb0 = B[0] if pair == 0 else B[2]
                    sb1 = B[1] if pair == 0 else B[3]
                    c.w("te", "sc", qT_mark)
                    wait_mark("te", f"bank{0 if pair == 0 else 2}")
                    wait_mark("te", f"bank{1 if pair == 0 else 3}")
                    c.do("te", lambda e, sb0=sb0, cchunk=cchunk, kc=kc: e.matmul(
                        sb0[:],
                        kT[0:64, cchunk * S + kc * 128: cchunk * S + kc * 128 + 128],
                        qTq[0:64, cchunk * 512: cchunk * 512 + 512],
                        start=True, stop=True), inc="pe")
                    c.do("te", lambda e, sb1=sb1, cchunk=cchunk, kc=kc: e.matmul(
                        sb1[:],
                        kT[64:128, cchunk * S + kc * 128: cchunk * S + kc * 128 + 128],
                        qTq[64:128, cchunk * 512: cchunk * 512 + 512],
                        start=True, stop=True), inc="pe")
                    sc_mark = c.cnt["pe"]
                    c.w("sc", "pe", sc_mark)
                    wait_mark("sc", f"pT{0 if pair == 0 else 2}")
                    wait_mark("sc", f"pT{1 if pair == 0 else 3}")
                    c.do("sc", lambda e, pair=pair: e.activation(
                        pTT[:, pair * 1024:(pair + 1) * 1024], SDP[pair][:],
                        AF.Exp, scale=0.125), inc="sc")
                    exp_mark = c.cnt["sc"]
                    mark(f"bank{0 if pair == 0 else 2}", "sc", exp_mark)
                    mark(f"bank{1 if pair == 0 else 3}", "sc", exp_mark)
                    return exp_mark

                def emit_av(hp, kc, exp_mark):
                    pair = 0 if kc % 2 == 0 else 1
                    c.w("te", "sc", exp_mark)
                    for hi in range(2):
                        h = 2 * hp + hi
                        vbase = kc * (H * 65) + h * 65
                        pp = pTT[:, pair * 1024 + hi * 512: pair * 1024 + hi * 512 + 512]
                        ob = B[4 + hi]
                        c.do("te", lambda e, pp=pp, ob=ob, vbase=vbase, kc=kc: e.matmul(
                            ob[0:65, :], vaug[:, vbase:vbase + 65], pp,
                            start=(kc == 0), stop=(kc == TT - 1)),
                            inc="pe" if (kc == TT - 1 and hi == 1) else None)
                    if kc == TT - 1:
                        av_mark = c.cnt["pe"]
                        mark(f"pT{2 * pair}", "pe", av_mark)
                        mark(f"pT{2 * pair + 1}", "pe", av_mark)
                        c.w("ve", "pe", av_mark)
                        par = hp % 2
                        wait_mark("ve", f"o_full{par}")
                        c.do("ve", lambda e, par=par: e.tensor_copy(
                            o_full[par][0:65, 0:512], B[4][0:65, :]))
                        c.do("ve", lambda e, par=par: e.tensor_copy(
                            o_full[par][0:65, 512:1024], B[5][0:65, :]), inc="ve")
                        oc_mark = c.cnt["ve"]
                        mark("bank4", "ve", oc_mark)
                        mark("bank5", "ve", oc_mark)
                        return oc_mark
                    else:
                        av_part = c.cnt["pe"]
                        mark(f"pT{2 * pair}", "pe", av_part)
                        mark(f"pT{2 * pair + 1}", "pe", av_part)
                        return None

                def emit_normalize(hp, oc_mark):
                    par = hp % 2
                    cchunk = hp
                    c.w("ve", "ve", oc_mark)
                    for hi in range(2):
                        h = 2 * hp + hi
                        rg = 64 * (h % 2)
                        sl = slice(hi * 512, hi * 512 + 512)
                        c.do("ve", lambda e, par=par, sl=sl: e.reciprocal(
                            epi_t1[0:1, sl], o_full[par][64:65, sl]),
                            inc="ve", flush=True)
                        rec_m = c.cnt["ve"]
                        bb = B[6] if hi == 0 else B[7]
                        c.w("te", "ve", rec_m)
                        wait_mark("te", f"bank{6 + hi}")
                        c.do("te", lambda e, bb=bb, sl=sl: e.matmul(
                            bb[0:64, :], ones_rf[:, 0:64], epi_t1[0:1, sl],
                            start=True, stop=True), inc="pe")
                        bc_m = c.cnt["pe"]
                        c.w("ve", "pe", bc_m)
                        c.do("ve", lambda e, rg=rg, cchunk=cchunk, par=par, sl=sl, bb=bb: e.tensor_mul(
                            oT[rg:rg + 64, cchunk * 512:(cchunk + 1) * 512],
                            o_full[par][0:64, sl], bb[0:64, :]), inc="ve")
                        mark(f"bank{6 + hi}", "ve", c.cnt["ve"])
                    mark(f"o_full{par}", "ve", c.cnt["ve"])

                pend_norm = None
                for hp in range(8):
                    exp_marks = {}
                    for kc in range(TT):
                        exp_marks[kc] = emit_scores(hp, kc)
                        if kc == 1 and pend_norm is not None:
                            emit_normalize(*pend_norm)
                            pend_norm = None
                        if kc >= 1:
                            emit_av(hp, kc - 1, exp_marks[kc - 1])
                    oc = emit_av(hp, TT - 1, exp_marks[TT - 1])
                    pend_norm = (hp, oc)
                emit_normalize(*pend_norm)
                pend_norm = None
                oT_mark = c.cnt["ve"]
                if s == 0 and qt == 0:
                    tap("dbg_oT", oT[:], "ve", oT_mark)

                for j in range(4):
                    tt = qt * 4 + j
                    row0 = s * S + tt * 128
                    c.w("te", "ve", oT_mark)
                    c.w("te", "dW", wqo_loaded)
                    wait_mark("te", "bank0")
                    wait_mark("te", "bank1")
                    for dh in range(2):
                        bk = B[dh]
                        for cc in range(8):
                            c.do("te", lambda e, dh=dh, cc=cc, bk=bk, j=j: e.matmul(
                                bk[:],
                                oT[:, cc * 512 + j * 128: cc * 512 + j * 128 + 128],
                                wsl1[:, cc * D + dh * 512: cc * D + dh * 512 + 512],
                                start=(cc == 0), stop=False))
                        c.do("te", lambda e, dh=dh, bk=bk: e.matmul(
                            bk[:], ones_r[:], bo_row[:, dh * 512:(dh + 1) * 512],
                            start=False, stop=True),
                            inc="pe" if dh == 1 else None)
                    oproj_mark = c.cnt["pe"]

                    xe = xe_f[j % 2]
                    wait_mark("sy", f"xe{j % 2}")
                    c.do("sy", lambda e, s=s, tt=tt, xe=xe: e.dma_start(
                        out=xe[:], in_=dx[s, tt * 128:(tt + 1) * 128, :]),
                        inc="dXb", inc_n=16)
                    xe_ld = c.cnt["dXb"]

                    c.w("ve", "pe", oproj_mark)
                    c.w("ve", "dXb", xe_ld)
                    wait_mark("ve", "epi")
                    for dh in range(2):
                        c.do("ve", lambda e, dh=dh, xe=xe: e.tensor_add(
                            epi_z[:, dh * 512:(dh + 1) * 512], B[dh][:],
                            xe[:, dh * 512:(dh + 1) * 512]),
                            inc="ve" if dh == 1 else None)
                    z_mark = c.cnt["ve"]
                    mark("bank0", "ve", z_mark)
                    mark("bank1", "ve", z_mark)
                    if s == 0 and qt == 0 and j == 0:
                        tap("dbg_z", epi_z[:], "ve", z_mark)

                    def emit_ln(src, dst, eps, col, affine):
                        entry_ve = c.cnt["ve"]
                        c.w("ve", "sc", c.cnt["sc"])
                        c.do("ve", lambda e: e.tensor_reduce(
                            stats[:, col:col + 1], src[:], mybir.AxisListType.X,
                            OP.add), flush=True)
                        c.w("sc", "ve", entry_ve)
                        c.do("sc", lambda e: e.activation(
                            epi_t1[:], src[:], AF.Square,
                            accum_out=stats[:, col + 1:col + 2]), inc="sc")
                        sq_m = c.cnt["sc"]
                        c.do("ve", lambda e: e.tensor_scalar(
                            stats[:, col + 2:col + 3], stats[:, col:col + 1],
                            1.0 / D, None, OP.mult), flush=True)
                        c.do("ve", lambda e: e.tensor_scalar(
                            stats[:, col + 3:col + 4], stats[:, col + 2:col + 3],
                            stats[:, col + 2:col + 3], None, OP.mult), flush=True)
                        c.w("ve", "sc", sq_m)
                        c.do("ve", lambda e: e.tensor_scalar(
                            stats[:, col + 1:col + 2], stats[:, col + 1:col + 2],
                            1.0 / D, stats[:, col + 3:col + 4], OP.mult, OP.subtract),
                            flush=True)
                        c.do("ve", lambda e: e.tensor_scalar(
                            stats[:, col + 1:col + 2], stats[:, col + 1:col + 2],
                            eps, None, OP.add), inc="ve", flush=True)
                        v_m = c.cnt["ve"]
                        c.w("sc", "ve", v_m)
                        c.do("sc", lambda e: e.activation(
                            stats[:, col + 1:col + 2], stats[:, col + 1:col + 2],
                            AF.Ln), flush=True)
                        c.do("sc", lambda e: e.activation(
                            stats[:, col + 1:col + 2], stats[:, col + 1:col + 2],
                            AF.Exp, scale=-0.5), inc="sc")
                        rstd_m = c.cnt["sc"]
                        c.w("ve", "sc", rstd_m)
                        if affine:
                            c.do("ve", lambda e: e.tensor_scalar(
                                epi_t1[:], src[:], stats[:, col + 2:col + 3],
                                stats[:, col + 1:col + 2], OP.subtract, OP.mult))
                            c.do("ve", lambda e: e.tensor_mul(epi_t1[:], epi_t1[:], gm_bc[:]))
                            c.do("ve", lambda e: e.tensor_add(dst[:], epi_t1[:], bt_bc[:]),
                                 inc="ve")
                        else:
                            c.do("ve", lambda e: e.tensor_scalar(
                                dst[:], src[:], stats[:, col + 2:col + 3],
                                stats[:, col + 1:col + 2], OP.subtract, OP.mult),
                                inc="ve")
                        return c.cnt["ve"]

                    if dbg_store == "z":
                        c.do("ve", lambda e: e.tensor_copy(x1f_st[2][:], epi_z[:]), inc="ve")
                    emit_ln(epi_z, epi_z, 1e-6, 0, affine=True)
                    if dbg_store == "abo":
                        c.do("ve", lambda e: e.tensor_copy(x1f_st[2][:], epi_z[:]), inc="ve")
                    c.do("ve", lambda e, xe=xe: e.tensor_add(epi_z[:], epi_z[:], xe[:]),
                         inc="ve")
                    if dbg_store == "u":
                        c.do("ve", lambda e: e.tensor_copy(x1f_st[2][:], epi_z[:]), inc="ve")
                    if dbg_store is not None:
                        dsm = c.cnt["ve"]
                        c.w("sy", "ve", dsm)
                        wait_mark("sy", "x1dbg")
                        c.do("sy", lambda e, row0=row0: e.dma_start(
                            out=d_x1f[row0:row0 + 128, :], in_=x1f_st[2][:]),
                            inc="dX1", inc_n=16)
                        mark("x1dbg", "dX1", c.cnt["dX1"])
                    mark(f"xe{j % 2}", "ve", c.cnt["ve"])
                    if s == 0 and qt == 0 and j == 0:
                        tap("dbg_u", epi_z[:], "ve", c.cnt["ve"])
                    x1buf = x1f_st[j % 2]
                    wait_mark("ve", f"x1f{j % 2}")
                    x1_m = emit_ln(epi_z, x1buf, 1e-5, 4, affine=False)
                    mark("epi", "ve", x1_m)
                    if s == 0 and qt == 0 and j == 0:
                        tap("dbg_x1t", x1buf[:], "ve", x1_m)
                        tap("dbg_stats", stats[:], "ve", x1_m)
                    c.w("sc", "ve", x1_m)
                    wait_mark("sc", f"x1b{j % 2}")
                    x1b = x1b_st[j % 2]
                    c.do("sc", lambda e, x1b=x1b, x1buf=x1buf: e.copy(x1b[:], x1buf[:]),
                         inc="sc")
                    x1b_m = c.cnt["sc"]
                    c.w("sy", "ve", x1_m)
                    if dbg_store is None:
                        c.do("sy", lambda e, x1buf=x1buf, row0=row0: e.dma_start(
                            out=d_x1f[row0:row0 + 128, :], in_=x1buf[:]),
                            inc="dX1", inc_n=16)
                    c.w("gp", "sc", x1b_m)
                    c.do("gp", lambda e, x1b=x1b, row0=row0: e.dma_start(
                        out=d_x1b[row0:row0 + 128, :], in_=x1b[:]), inc="dX1b", inc_n=16)
                    mark(f"x1f{j % 2}", "dX1", c.cnt["dX1"])
                    mark(f"x1b{j % 2}", "dX1b", c.cnt["dX1b"])

                    c.w("te", "sc", x1b_m)
                    wait_mark("te", "bank2")
                    wait_mark("te", "bank3")
                    for jj in range(8):
                        bk = B[2] if jj < 4 else B[3]
                        col = (jj % 4) * 128
                        c.do("te", lambda e, jj=jj, bk=bk, col=col, x1b=x1b: e.matmul(
                            bk[:, col:col + 128], x1b[:, jj * 128:(jj + 1) * 128],
                            ident_b[:], start=True, stop=True),
                            inc="pe" if jj == 7 else None)
                    gtp_mark = c.cnt["pe"]
                    c.w("ve", "pe", gtp_mark)
                    wait_mark("ve", "x1T_t")
                    c.do("ve", lambda e: e.tensor_copy(x1T_t[:, 0:512], B[2][:]))
                    c.do("ve", lambda e: e.tensor_copy(x1T_t[:, 512:1024], B[3][:]), inc="ve")
                    g_xT = c.cnt["ve"]
                    mark("bank2", "ve", g_xT)
                    mark("bank3", "ve", g_xT)
                    c.w("te", "ve", g_xT)
                    wait_mark("te", "bank6")
                    for dinc in range(8):
                        c.do("te", lambda e, dinc=dinc: e.matmul(
                            B[6][:, 0:8], x1T_t[:, dinc * 128:(dinc + 1) * 128],
                            gate_b[:, dinc * 8:(dinc + 1) * 8],
                            start=(dinc == 0), stop=(dinc == 7)),
                            inc="pe" if dinc == 7 else None)
                    mark("x1T_t", "pe", c.cnt["pe"])
                    gsc_mark = c.cnt["pe"]
                    c.w("sc", "pe", gsc_mark)
                    c.w("sc", "ve", c.cnt["ve"])
                    c.do("sc", lambda e: e.activation(
                        stats[:, 8:16], B[6][:, 0:8], AF.Exp,
                        accum_out=stats[:, 16:17]), inc="sc")
                    ge_m = c.cnt["sc"]
                    mark("bank6", "sc", ge_m)
                    c.w("ve", "sc", ge_m)
                    c.do("ve", lambda e: e.reciprocal(stats[:, 17:18], stats[:, 16:17]),
                         flush=True)
                    ptcol = (s * TT + tt) * 8
                    c.do("ve", lambda e, ptcol=ptcol: e.tensor_scalar(
                        probs_tok[:, ptcol:ptcol + 8], stats[:, 8:16],
                        stats[:, 17:18], None, OP.mult), inc="ve")
                    pt_m = c.cnt["ve"]
                    c.w("te", "ve", pt_m)
                    wait_mark("te", "bank7")
                    c.do("te", lambda e, ptcol=ptcol: e.matmul(
                        B[7][0:8, 0:128], probs_tok[:, ptcol:ptcol + 8],
                        ident_f[:], start=True, stop=True), inc="pe")
                    pe_tr = c.cnt["pe"]
                    c.w("sc", "pe", pe_tr)
                    base_p = 0 if s == 0 else 32
                    c.do("sc", lambda e, base_p=base_p, tt=tt: e.copy(
                        probs_en[base_p:base_p + 8, tt * 128:(tt + 1) * 128],
                        B[7][0:8, 0:128]), inc="sc")
                    mark("bank7", "sc", c.cnt["sc"])

            b_done = {k: c.cnt[k] for k in ("pe", "sc", "ve")}
            mark("wsl", "pe", b_done["pe"])

            if f"C{s}" in _inc:
                pending_c[s] = gen_stage_c(s)
                if s == NSEQ - 1:
                    pass
            if s == 0:
                hoist_state["x1b0"] = c.cnt["dX1b"]
            if s == 1:
                drain_c(0)
                if "D" in _inc:
                    c.w("gp", "dMt", c.cnt["dMt"])
                    c.do("gp", lambda e: e.dma_start(
                        out=m_sb[:, 0:64],
                        in_=d_mt.rearrange("(q k) c -> q k c", q=128)[:, 0:32, :]),
                        inc="dMs", inc_n=16)
                    c.w("ve", "dMs", c.cnt["dMs"])
                    c.do("ve", lambda e: e.tensor_scalar(
                        m_i32[:, 0:32],
                        m_sb[:, 0:64].rearrange("p (k c) -> p k c", c=2)[:, :, 0],
                        0.0, None, OP.add), inc="ve")
                    hoist_mi0 = c.cnt["ve"]
                    c.w("gp", "ve", hoist_mi0)
                    c.w("gp", "dX1b", hoist_state["x1b0"])
                    for st in range(4):
                        colx = st
                        c.do("gp", lambda e, colx=colx, st=st: e.indirect_dma_start(
                            out=xin4[:, st * D:(st + 1) * D], out_offset=None,
                            in_=d_x1b,
                            in_offset=IndirectOffsetOnAxis(m_i32[:, colx:colx + 1], 0)),
                            inc="dGa", inc_n=16)
                    hoist_state.update(mi0=hoist_mi0, ga=c.cnt["dGa"])
                    for ce in (0, 1):
                        for hh in range(2):
                            for cg in range(hh * 4, hh * 4 + 4):
                                c.do("gp", lambda e, ce=ce, cg=cg: e.dma_start(
                                    out=d_w1b[ce].rearrange(
                                        "p (g k f) -> p g k f", g=8, k=8)[:, cg],
                                    in_=dw1[ce].rearrange(
                                        "(k p) (g f) -> p g k f", p=128, f=512)
                                    [:, cg]),
                                    inc="dWc", inc_n=16)
                        wc[ce] = c.cnt["dWc"]

        mt_n0_done = None
        attn_done = {k: c.cnt[k] for k in ("pe", "sc", "ve")}

        if "D" not in _inc:
            phase_gate()
            return c.cnt
        phase_gate()
        mi_m = [hoist_state["mi0"], None]
        ga_pending = hoist_state["ga"]
        xin4_free = None

        def emit_gathers(ge, gn):
            c.w("gp", "ve", mi_m[gn])
            c.w("gp", "dX1b", c.cnt["dX1b"])
            if xin4_free is not None:
                c.w("gp", *xin4_free)
            for st in range(4):
                colx = (gn * 8 + ge) * 4 + st
                c.do("gp", lambda e, colx=colx, st=st: e.indirect_dma_start(
                    out=xin4[:, st * D:(st + 1) * D], out_offset=None,
                    in_=d_x1b,
                    in_offset=IndirectOffsetOnAxis(m_i32[:, colx:colx + 1], 0)),
                    inc="dGa", inc_n=16)
            return c.cnt["dGa"]

        x1b_all = c.cnt["dX1b"]

        hT_free = None
        w1_slot_free = [None, None]
        cast_iter = iter([(ce, hh) for ce in range(2, NE) for hh in (0, 1)])
        pref_next = False
        w2_free = None
        xinT_free = None
        scat_chain = {0: 0, 1: 0}

        pair_seq = ([(0, 0), (1, 0), (0, 1), (1, 1)]
                    + [(pe_, pn_) for pe_ in range(2, NE) for pn_ in (0, 1)])
        last_e = None
        for idx in range(len(pair_seq)):
            e_, n = pair_seq[idx]
            if e_ != last_e:
                last_e = e_
                if w2_free is not None:
                    c.w("gp", *w2_free)
                c.w("gp", "pe", attn_done["pe"])
                for g in range(8):
                    c.do("gp", lambda e, e_=e_, g=g: e.dma_start(
                        out=w2b[:].rearrange("p (f d) -> p f d", d=D)
                        [:, g * 4:(g + 1) * 4, :],
                        in_=dw2[e_].rearrange("(f p) d -> p f d", p=128)
                        [:, g * 4:(g + 1) * 4, :]),
                        inc="dW2", inc_n=16)
                w2_ld = c.cnt["dW2"]

            if True:
                nst4 = [(n * 8 + e_) * 4 + st for st in range(4)]
                if n == 1 and mi_m[1] is None:
                    drain_c(1)
                    c.w("gp", "dMt", c.cnt["dMt"])
                    c.do("gp", lambda e: e.dma_start(
                        out=m_sb[:, 64:128],
                        in_=d_mt.rearrange("(q k) c -> q k c", q=128)[:, 32:64, :]),
                        inc="dMs", inc_n=16)
                    c.w("ve", "dMs", c.cnt["dMs"])
                    c.do("ve", lambda e: e.tensor_scalar(
                        m_i32[:, 32:64],
                        m_sb[:, 64:128].rearrange("p (k c) -> p k c", c=2)[:, :, 0],
                        0.0, None, OP.add), inc="ve")
                    mi_m[1] = c.cnt["ve"]
                    ga_pending = emit_gathers(e_, n)
                if xinT_free is not None:
                    c.w("te", *xinT_free)
                c.w("te", "dGa", ga_pending)
                c.w("te", "pe", attn_done["pe"])
                c.w("te", "dW", init_w)
                for st in range(4):
                    wait_mark("te", "bank0")
                    wait_mark("te", "bank1")
                    for jj in range(8):
                        bk = B[0] if jj < 4 else B[1]
                        col = (jj % 4) * 128
                        c.do("te", lambda e, jj=jj, bk=bk, col=col, st=st: e.matmul(
                            bk[:, col:col + 128],
                            xin4[:, st * D + jj * 128: st * D + jj * 128 + 128],
                            ident_b[:], start=True, stop=True),
                            inc="pe" if jj == 7 else None)
                    tp_m = c.cnt["pe"]
                    c.w("ve", "pe", tp_m)
                    c.w("ve", "sc", attn_done["sc"])
                    for half in range(2):
                        c.do("ve", lambda e, half=half, st=st: e.tensor_copy(
                            xinT[:].rearrange("p (k w) -> p k w", w=512)
                            [:, half * 4:(half + 1) * 4, st * 128:(st + 1) * 128],
                            B[half][:].rearrange("p (k w) -> p k w", w=128)),
                            inc="ve" if half == 1 else None)
                    xin_m = c.cnt["ve"]
                    mark("bank0", "ve", xin_m)
                    mark("bank1", "ve", xin_m)
                xinT_ready = ("ve", c.cnt["ve"])
                xin4_free = ("pe", c.cnt["pe"])

                pref_pair, pref_next = pref_next, False
                gelu_cnt_start = c.cnt["sc"]
                for fc in range(32):
                    if idx < 2:
                        drive_c(1, 2)
                    g = fc // 4
                    if fc % 4 == 0 and not (pref_pair and g < 2):
                        slot = g % 2
                        if w1_slot_free[slot] is not None:
                            c.w("sy", *w1_slot_free[slot])
                        c.w("sy", "dWc", wc[e_])
                        c.do("sy", lambda e, e_=e_, g=g, slot=slot: e.dma_start(
                            out=w1g[slot][:].rearrange("p (k w) -> p k w", w=512),
                            in_=d_w1b[e_].rearrange(
                                "p (g k f) -> p g k f", g=8, k=8)[:, g]),
                            inc="dW1a" if slot == 0 else "dW1b", inc_n=16)
                    slot = g % 2
                    wsem = "dW1a" if slot == 0 else "dW1b"
                    bk = B[2 + (fc % 2)]
                    c.w("te", wsem, c.cnt[wsem])
                    c.w("te", *xinT_ready)
                    wait_mark("te", f"bank{2 + (fc % 2)}")
                    for dinc in range(8):
                        c.do("te", lambda e, fc=fc, dinc=dinc, bk=bk, slot=slot: e.matmul(
                            bk[:],
                            w1g[slot][:, dinc * 512 + (fc % 4) * 128:
                                      dinc * 512 + (fc % 4) * 128 + 128],
                            xinT[:, dinc * 512: dinc * 512 + 512],
                            start=(dinc == 0), stop=(dinc == 7)),
                            inc="pe" if dinc == 7 else None)
                    h_m = c.cnt["pe"]
                    if fc % 4 == 3:
                        w1_slot_free[slot] = ("pe", h_m)
                    c.w("sc", "pe", h_m)
                    c.do("sc", lambda e, fc=fc, bk=bk: e.activation(
                        hT[:, fc * 512:(fc + 1) * 512], bk[:], AF.Gelu_apprx_tanh),
                        inc="sc")
                    mark(f"bank{2 + (fc % 2)}", "sc", c.cnt["sc"])
                h_done = c.cnt["sc"]
                xinT_free = ("pe", c.cnt["pe"])
                nxt = pair_seq[idx + 1] if idx + 1 < len(pair_seq) else (NE, 0)
                if nxt[0] < NE:
                    if nxt != (0, 1):
                        ga_pending = emit_gathers(*nxt)
                    for g2 in (0, 1):
                        if w1_slot_free[g2] is not None:
                            c.w("sy", *w1_slot_free[g2])
                        c.w("sy", "dWc", wc[nxt[0]])
                        c.do("sy", lambda e, ne_=nxt[0], g=g2, slot=g2: e.dma_start(
                            out=w1g[slot][:].rearrange("p (k w) -> p k w", w=512),
                            in_=d_w1b[ne_].rearrange(
                                "p (g k f) -> p g k f", g=8, k=8)[:, g]),
                            inc="dW1a" if g2 == 0 else "dW1b", inc_n=16)
                    pref_next = True
                if idx > 0:
                    nxt_cast = next(cast_iter, None)
                    if nxt_cast is not None:
                        ce, hh = nxt_cast
                        for cg in range(hh * 4, hh * 4 + 4):
                            c.do("gp", lambda e, ce=ce, cg=cg: e.dma_start(
                                out=d_w1b[ce].rearrange(
                                    "p (g k f) -> p g k f", g=8, k=8)[:, cg],
                                in_=dw1[ce].rearrange(
                                    "(k p) (g f) -> p g k f", p=128, f=512)
                                [:, cg]),
                                inc="dWc", inc_n=16)
                        if hh == 1:
                            wc[ce] = c.cnt["dWc"]

                for st in range(4):
                    if idx < 2:
                        drive_c(1, 4)
                    colx = nst4[st]
                    yb = (B[4], B[5]) if st % 2 == 0 else (B[6], B[7])
                    ybn = ("bank4", "bank5") if st % 2 == 0 else ("bank6", "bank7")
                    c.w("te", "sc", h_done)
                    c.w("te", "dW2", w2_ld)
                    wait_mark("te", ybn[0])
                    wait_mark("te", ybn[1])
                    for fc in range(32):
                        for dh in range(2):
                            c.do("te", lambda e, st=st, dh=dh, fc=fc, yb=yb: e.matmul(
                                yb[dh][:],
                                hT[:, fc * 512 + st * 128: fc * 512 + st * 128 + 128],
                                w2b[:, fc * D + dh * 512: fc * D + dh * 512 + 512],
                                start=(fc == 0), stop=(fc == 31)),
                                inc="pe" if (dh == 1 and fc == 31) else None)
                    y_m = c.cnt["pe"]
                    ywb = y_w[st % 2]
                    c.w("sc", "pe", y_m)
                    wait_mark("sc", f"y_w{st % 2}")
                    for dh in range(2):
                        c.do("sc", lambda e, dh=dh, ywb=ywb, yb=yb, colx=colx: e.activation(
                            ywb[:, dh * 512:(dh + 1) * 512], yb[dh][:], AF.Copy,
                            scale=m_sb[:, 2 * colx + 1: 2 * colx + 2]),
                            inc="sc" if dh == 1 else None)
                    yev_m = c.cnt["sc"]
                    mark(ybn[0], "sc", yev_m)
                    mark(ybn[1], "sc", yev_m)
                    csem = "dScA" if n == 0 else "dScB"
                    c.w("gp", "sc", yev_m)
                    c.w("gp", "dAcc", 16 * (NROW // 128))
                    c.w("gp", csem, scat_chain[n])
                    c.do("gp", lambda e, ywb=ywb, colx=colx: e.indirect_dma_start(
                        out=d_acc,
                        out_offset=IndirectOffsetOnAxis(m_i32[:, colx:colx + 1], 0),
                        in_=ywb[:], in_offset=None,
                        bounds_check=bnd["acc"], oob_is_err=False,
                        compute_op=OP.add),
                        inc=csem, inc_n=16)
                    mark(f"y_w{st % 2}", csem, c.cnt[csem])
                scat_chain[n] = c.cnt["dScA" if n == 0 else "dScB"]
                w2_free = ("pe", c.cnt["pe"])

        if "E" not in _inc:
            phase_gate()
            return c.cnt
        phase_gate()
        x1f_all = c.cnt["dX1"]
        for r in range(NROW // 128):
            pb = r % 2
            col = pb * 2
            wait_mark("sy", f"e_ld{pb}")
            c.w("sy", "dScA", scat_chain[0])
            c.w("sy", "dScB", scat_chain[1])
            c.w("sy", "dX1", x1f_all)
            c.w("sy", "dMt", c.cnt["dMt"])
            c.do("sy", lambda e, r=r, pb=pb: e.dma_start(
                out=e_x1[pb][:], in_=d_x1f[r * 128:(r + 1) * 128, :]), inc="dEx", inc_n=16)
            c.do("sy", lambda e, r=r, pb=pb: e.dma_start(
                out=e_ac[pb][:], in_=d_acc[r * 128:(r + 1) * 128, :]), inc="dEx", inc_n=16)
            ld_m = c.cnt["dEx"]
            c.w("ve", "dEx", ld_m)
            c.w("ve", "pe", c.cnt["pe"])
            wait_mark("ve", f"e_u{pb}")
            c.do("ve", lambda e, pb=pb: e.tensor_copy(e_af[pb][:], e_ac[pb][:]))
            c.do("ve", lambda e, pb=pb: e.tensor_add(e_u[pb][:], e_x1[pb][:], e_af[pb][:]),
                 inc="ve")
            mark(f"e_ld{pb}", "ve", c.cnt["ve"])
            wait_mark("ve", f"e_st{pb}")
            c.do("ve", lambda e, pb=pb, col=col: e.tensor_reduce(
                stats[:, col:col + 1], e_u[pb][:], mybir.AxisListType.X, OP.add),
                flush=True)
            c.do("ve", lambda e, col=col: e.tensor_scalar(
                stats[:, col:col + 1], stats[:, col:col + 1], -1.0 / D, None, OP.mult),
                inc="ve")
            nm = c.cnt["ve"]
            c.w("sc", "ve", nm)
            wait_mark("sc", f"e_af{pb}")
            c.do("sc", lambda e, pb=pb, col=col: e.activation(
                e_af[pb][:], e_u[pb][:], AF.Identity, bias=stats[:, col:col + 1]),
                inc="sc")
            c.do("sc", lambda e, pb=pb, col=col: e.activation(
                e_u[pb][:], e_af[pb][:], AF.Square,
                accum_out=stats[:, col + 1:col + 2]), inc="sc")
            mark(f"e_u{pb}", "sc", c.cnt["sc"])
            c.w("ve", "sc", c.cnt["sc"])
            c.do("ve", lambda e, col=col: e.tensor_scalar(
                stats[:, col + 1:col + 2], stats[:, col + 1:col + 2],
                1.0 / D, 1e-5, OP.mult, OP.add), inc="ve")
            c.w("sc", "ve", c.cnt["ve"])
            c.do("sc", lambda e, col=col: e.activation(
                stats[:, col + 1:col + 2], stats[:, col + 1:col + 2], AF.Ln),
                flush=True)
            c.do("sc", lambda e, col=col: e.activation(
                stats[:, col + 1:col + 2], stats[:, col + 1:col + 2],
                AF.Exp, scale=-0.5), inc="sc")
            c.w("ve", "sc", c.cnt["sc"])
            wait_mark("ve", f"e_o{pb}")
            c.do("ve", lambda e, pb=pb, col=col: e.tensor_scalar(
                e_o[pb][:], e_af[pb][:], stats[:, col + 1:col + 2], None, OP.mult),
                inc="ve")
            o_m = c.cnt["ve"]
            mark(f"e_af{pb}", "ve", o_m)
            mark(f"e_st{pb}", "ve", o_m)
            c.w("sy", "ve", o_m)
            s_ = r // TT
            t0 = (r % TT) * 128
            c.do("sy", lambda e, pb=pb, s_=s_, t0=t0: e.dma_start(
                out=dout[s_, t0:t0 + 128, :], in_=e_o[pb][:]), inc="dOut", inc_n=16)
            mark(f"e_o{pb}", "dOut", c.cnt["dOut"])

        c.w("sy", "dOut", c.cnt["dOut"])
        c.w("gp", "dTap", c.cnt["dTap"])

        return c.cnt

    with nc.Block() as block:
        final_counts = {}

        def emit(eng_name):
            def f(engine):
                ctx = Ctx(nc, sems, engines, eng_name, first=not final_counts)
                cnts = walk(ctx)
                if final_counts:
                    assert cnts == final_counts["c"], f"count drift on {eng_name}"
                else:
                    final_counts["c"] = cnts
            return f

        block.sync(emit("sy"))
        block.tensor(emit("te"))
        block.scalar(emit("sc"))
        block.vector(emit("ve"))
        block.gpsimd(emit("gp"))

    return nc


_NC_CACHE = {}


def _get_nc():
    if "nc" not in _NC_CACHE:
        _NC_CACHE["nc"] = build_nc()
    return _NC_CACHE["nc"]


def kernel(L0_attn_inputs, L0_attn_Wq, L0_attn_bq, L0_attn_Wk, L0_attn_bk,
           L0_attn_Wv, L0_attn_bv, L0_attn_Wo, L0_attn_bo, L0_attn_gamma,
           L0_attn_beta, L0_ffn_gate_w, L0_ffn_w1, L0_ffn_w2):
    x = np.ascontiguousarray(np.asarray(L0_attn_inputs, dtype=np.float32))
    ne4 = np.zeros((40, 1), np.float32)
    for p in range(8):
        ne4[p, 0] = p * 4
        ne4[32 + p, 0] = (8 + p) * 4
    common = {
        "Wq": np.ascontiguousarray(L0_attn_Wq, dtype=np.float32),
        "Wk": np.ascontiguousarray(L0_attn_Wk, dtype=np.float32),
        "Wv": np.ascontiguousarray(L0_attn_Wv, dtype=np.float32),
        "Wo": np.ascontiguousarray(L0_attn_Wo, dtype=np.float32),
        "bqT": np.ascontiguousarray(np.asarray(L0_attn_bq, np.float32).reshape(8, 128).T),
        "bkT": np.ascontiguousarray(np.asarray(L0_attn_bk, np.float32).reshape(8, 128).T),
        "bv": np.ascontiguousarray(np.asarray(L0_attn_bv, np.float32).reshape(1, D)),
        "bo": np.ascontiguousarray(np.asarray(L0_attn_bo, np.float32).reshape(1, D)),
        "gamma": np.ascontiguousarray(np.asarray(L0_attn_gamma, np.float32).reshape(1, D)),
        "beta": np.ascontiguousarray(np.asarray(L0_attn_beta, np.float32).reshape(1, D)),
        "gate_w": np.ascontiguousarray(L0_ffn_gate_w, dtype=np.float32),
        "w1": np.ascontiguousarray(L0_ffn_w1, dtype=np.float32),
        "w2": np.ascontiguousarray(L0_ffn_w2, dtype=np.float32),
        "ident": np.eye(128, dtype=np.float32),
        "ne4": ne4,
    }
    in_maps = []
    for core in range(8):
        m = dict(common)
        m["x"] = np.ascontiguousarray(x[2 * core:2 * core + 2])
        m["xt"] = np.ascontiguousarray(m["x"].transpose(0, 2, 1))
        in_maps.append(m)
    nc = _get_nc()
    res = run_bass_kernel_spmd(nc, in_maps, core_ids=list(range(8)))
    out = np.concatenate([r["out"] for r in res.results], axis=0)
    return out.reshape(16, S, D).astype(np.float32)


# version 15
# speedup vs baseline: 1.0256x; 1.0256x over previous
import numpy as np
from contextlib import ExitStack

import concourse.bass as bass
import concourse.mybir as mybir
from concourse.bass_utils import run_bass_kernel_spmd
from concourse.bass_core_rust import IndirectOffsetOnAxis

F32 = mybir.dt.float32
BF16 = mybir.dt.bfloat16
I32 = mybir.dt.int32
AF = mybir.ActivationFunctionType
OP = mybir.AluOpType

S = 2048
D = 1024
H = 16
HD = 64
NSEQ = 2
TT = S // 128
NE = 8
KSEL = 512
FF = 4096
NROW = NSEQ * S

ENGS = ("sy", "te", "sc", "ve", "gp")


class Ctx:

    def __init__(self, nc, sems, engines, cur, first):
        self.nc = nc
        self.sems = sems
        self.eng = engines
        self.cur = cur
        self.first = first
        self.cnt = {k: 0 for k in sems}
        self._lastw = {}

    def w(self, eng, sem, val=None):
        if val is None:
            val = self.cnt[sem]
        if val <= 0:
            return
        key = (eng, sem)
        if self._lastw.get(key, -1) >= val:
            return
        self._lastw[key] = val
        if self.cur == eng:
            self.eng[eng].wait_ge(self.sems[sem], val)

    def do(self, eng, fn, inc=None, inc_n=1, flush=False):
        if flush:
            fs = {"ve": "fve", "sc": "fsc"}[eng]
            self.cnt[fs] += 1
        if self.cur == eng:
            ins = fn(self.eng[eng])
            if flush:
                ins.then_inc(self.sems[fs], 1)
                self.eng[eng].wait_ge(self.sems[fs], self.cnt[fs])
                if inc is not None:
                    self.eng[eng].sem_inc(self.sems[inc], inc_n)
            elif inc is not None:
                ins.then_inc(self.sems[inc], inc_n)
        if inc is not None:
            self.cnt[inc] += inc_n
        return self.cnt[inc] if inc is not None else None


def build_nc(debug=(), stop_after=None, dbg_store=None):
    _ORDER = ["INIT", "A0", "B0", "C0", "A1", "B1", "C1", "D", "E"]
    _inc = set(_ORDER if stop_after is None else _ORDER[:_ORDER.index(stop_after) + 1])
    nc = bass.Bass("TRN2", target_bir_lowering=False, detect_race_conditions=False)

    dx = nc.dram_tensor("x", [NSEQ, S, D], F32, kind="ExternalInput").ap()
    dxt = nc.dram_tensor("xt", [NSEQ, D, S], F32, kind="ExternalInput").ap()
    dWq = nc.dram_tensor("Wq", [D, D], F32, kind="ExternalInput").ap()
    dWk = nc.dram_tensor("Wk", [D, D], F32, kind="ExternalInput").ap()
    dWv = nc.dram_tensor("Wv", [D, D], F32, kind="ExternalInput").ap()
    dWo = nc.dram_tensor("Wo", [D, D], F32, kind="ExternalInput").ap()
    dbqT = nc.dram_tensor("bqT", [128, 8], F32, kind="ExternalInput").ap()
    dbkT = nc.dram_tensor("bkT", [128, 8], F32, kind="ExternalInput").ap()
    dbv = nc.dram_tensor("bv", [1, D], F32, kind="ExternalInput").ap()
    dbo = nc.dram_tensor("bo", [1, D], F32, kind="ExternalInput").ap()
    dgm = nc.dram_tensor("gamma", [1, D], F32, kind="ExternalInput").ap()
    dbt = nc.dram_tensor("beta", [1, D], F32, kind="ExternalInput").ap()
    dgate = nc.dram_tensor("gate_w", [D, NE], F32, kind="ExternalInput").ap()
    dw1 = nc.dram_tensor("w1", [NE, D, FF], F32, kind="ExternalInput").ap()
    d_w1b = nc.dram_tensor("w1b", [NE, 128, 8 * 8 * 512], BF16, kind="Internal").ap()
    dw2 = nc.dram_tensor("w2", [NE, FF, D], F32, kind="ExternalInput").ap()
    dident = nc.dram_tensor("ident", [128, 128], F32, kind="ExternalInput").ap()
    dne4 = nc.dram_tensor("ne4", [40, 1], F32, kind="ExternalInput").ap()

    dout = nc.dram_tensor("out", [NSEQ, S, D], F32, kind="ExternalOutput").ap()

    def scratch(name, shape, dtype):
        kind = "ExternalOutput" if name in debug else "Internal"
        return nc.dram_tensor(name, shape, dtype, kind=kind).ap()

    dbg_taps = {}
    for nm, shp, dt_ in (("dbg_z", [128, D], F32), ("dbg_u", [128, D], F32),
                         ("dbg_x1t", [128, D], F32), ("dbg_stats", [128, 16], F32),
                         ("dbg_oT", [128, 8 * 512], BF16),
                         ("dbg_qT", [128, 8 * 512], BF16),
                         ("dbg_kT", [128, 8 * S], BF16),
                         ("dbg_vaug", [128, TT * H * 65], BF16)):
        if nm in debug:
            dbg_taps[nm] = nc.dram_tensor(nm, shp, dt_, kind="ExternalOutput").ap()

    d_x1f = scratch("x1f", [NROW, D], F32)
    d_x1b = scratch("x1b", [NROW, D], BF16)
    d_acc = scratch("accb", [NROW, D], BF16)
    d_mt = scratch("mt", [NE * 2 * KSEL, 2], F32)

    SB0 = ((nc.sbuf_base + 31) // 32) * 32
    SBTOP = nc.sbuf_top - 32
    off = [SB0]
    offsets = {}

    def at(name, shape, dtype, where=None):
        nbytes = int(np.prod(shape[1:])) * (2 if dtype == BF16 else 4)
        if where is None:
            o = off[0]
            off[0] = o + ((nbytes + 31) // 32) * 32
        else:
            o = where
        offsets[name] = o
        assert o + nbytes <= SBTOP, f"sbuf overflow {name}: {o}+{nbytes}"
        return nc.alloc_sbuf_tensor_at(name, shape, dtype, offset=o).ap(), o + nbytes

    ident_f, _ = at("ident_f", [128, 128], F32)
    ident_b, _ = at("ident_b", [128, 128], BF16)
    gate_b, _ = at("gate_b", [128, NE * 8], BF16)
    bqT, _ = at("bqT_s", [128, 8], F32)
    bkT, _ = at("bkT_s", [128, 8], F32)
    bv_row, _ = at("bv_row", [1, D], BF16)
    bo_row, _ = at("bo_row", [1, D], BF16)
    ones_r, _ = at("ones_r", [1, 128], BF16)
    ones_rf, _ = at("ones_rf", [1, 128], F32)
    gm_bc, _ = at("gm_bc", [128, D], F32)
    bt_bc, _ = at("bt_bc", [128, D], F32)
    zero_b, _ = at("zero_b", [128, D], BF16)
    probs_tok, _ = at("probs_tok", [128, NSEQ * TT * 8], F32)
    probs_en, _ = at("probs_en", [40, S], F32)
    ne4_sb, _ = at("ne4_sb", [40, 1], F32)
    stats, _ = at("stats", [128, 24], F32)
    small, _ = at("small", [40, 16], F32)
    smalli, _ = at("smalli", [40, 8], I32)
    m_sb, _ = at("m_sb", [128, 128], F32)
    gsrc, _ = at("gsrc", [128, 16], F32)
    gsrci, _ = at("gsrci", [128, 8], I32)
    m_i32, _ = at("m_i32", [128, 64], I32)
    PERS_END = off[0]

    wsl0, _ = at("wsl0", [128, 8 * D], BF16)
    wsl1, _ = at("wsl1", [128, 8 * D], BF16)
    A_W_END = off[0]
    kT, _ = at("kT", [128, 8 * S], BF16)
    vaug, _ = at("vaug", [128, TT * H * 65], BF16)
    A_KV_END = off[0]
    qTq, _ = at("qTq", [128, 8 * 512], BF16)
    oT, _ = at("oT", [128, 8 * 512], BF16)
    pTT, _ = at("pTT", [128, 2048], BF16)
    pT = [pTT[:, 0:512], pTT[:, 512:1024], pTT[:, 1024:1536], pTT[:, 1536:2048]]
    o_full = []
    for i in range(2):
        t, _ = at(f"o_full{i}", [128, 1024], F32)
        o_full.append(t)
    epi_z, _ = at("epi_z", [128, D], F32)
    epi_t1, _ = at("epi_t1", [128, D], F32)
    A_B_ONLY_END = off[0]
    xTq = []
    for i in range(2):
        t, _ = at(f"xTq{i}", [128, 8 * 512], BF16)
        xTq.append(t)
    xe_f = []
    for i in range(2):
        t, _ = at(f"xe_f{i}", [128, D], F32)
        xe_f.append(t)
    x1f_st = []
    x1b_st = []
    for i in range(2):
        t, _ = at(f"x1f_st{i}", [128, D], F32)
        x1f_st.append(t)
        t, _ = at(f"x1b_st{i}", [128, D], BF16)
        x1b_st.append(t)
    if dbg_store is not None:
        t, _ = at("x1f_st2", [128, D], F32)
        x1f_st.append(t)
    x1T_t, _ = at("x1T_t", [128, 8 * 128], BF16)
    xin4, _ = at("xin4", [128, 4 * D], BF16)
    A_END = off[0]

    offC = [None]

    def atC(name, shape, dtype):
        nbytes = int(np.prod(shape[1:])) * (2 if dtype == BF16 else 4)
        o = offC[0]
        offC[0] = o + ((nbytes + 31) // 32) * 32
        offsets[name] = o
        assert offC[0] <= SBTOP, f"sbuf overflow {name}"
        return nc.alloc_sbuf_tensor_at(name, shape, dtype, offset=o).ap()


    offF = [PERS_END]

    def atF(name, shape, dtype):
        nbytes = int(np.prod(shape[1:])) * (2 if dtype == BF16 else 4)
        o = offF[0]
        offF[0] = o + ((nbytes + 31) // 32) * 32
        offsets[name] = o
        assert offF[0] <= SBTOP, f"sbuf overflow {name}"
        return nc.alloc_sbuf_tensor_at(name, shape, dtype, offset=o).ap()

    hT = atF("hT", [128, 32 * 512], BF16)
    w2b = atF("w2b", [128, 32 * D], BF16)
    xinT = atF("xinT", [128, 8 * 512], BF16)
    xin_rows = atF("xin_rows", [128, D], BF16)
    y_w = []
    for i in range(2):
        y_w.append(atF(f"y_w{i}", [128, D], BF16))
    w1g = [atF("w1g0", [128, 8 * 512], BF16),
           atF("w1g1", [128, 8 * 512], BF16)]
    gm_row = atF("gm_row", [1, D], F32)
    bt_row = atF("bt_row", [1, D], F32)


    F_END = offF[0]
    assert F_END <= SBTOP

    offE = [F_END]

    def atE(name, shape, dtype):
        nbytes = int(np.prod(shape[1:])) * (2 if dtype == BF16 else 4)
        o = offE[0]
        offE[0] = o + ((nbytes + 31) // 32) * 32
        assert offE[0] <= SBTOP, f"sbuf overflow {name}"
        return nc.alloc_sbuf_tensor_at(name, shape, dtype, offset=o).ap()

    e_x1 = [atE("e_x1_0", [128, D], F32), atE("e_x1_1", [128, D], F32)]
    e_ac = [atE("e_ac_0", [128, D], BF16), atE("e_ac_1", [128, D], BF16)]
    e_af = [atE("e_af0", [128, D], F32), atE("e_af1", [128, D], F32)]
    e_u = [atE("e_u0", [128, D], F32), atE("e_u1", [128, D], F32)]
    e_o = [atE("e_o_0", [128, D], F32), atE("e_o_1", [128, D], F32)]

    def alloc_cset(tag, base):
        o = [base]

        def a(name, shape, dtype):
            nb = int(np.prod(shape[1:])) * (2 if dtype == BF16 else 4)
            off_ = o[0]
            o[0] = off_ + ((nb + 31) // 32) * 32
            assert o[0] <= SBTOP, f"sbuf overflow {name}{tag}"
            return nc.alloc_sbuf_tensor_at(f"{name}{tag}", shape, dtype, offset=off_).ap()

        cs = {
            "dummy": a("c_dummy", [40, S], BF16),
            "A": a("c_A", [40, S], BF16),
            "T": a("c_T", [40, S], BF16),
            "csS": a("c_csS", [40, 512], F32),
            "csT": a("c_csT", [40, 512], F32),
            "r": a("c_r", [40, 512], F32),
            "j": a("c_j", [40, 512], F32),
            "k1": a("c_k1", [40, 512], BF16),
            "keep": a("c_keep", [40, 512], I32),
            "c255": a("c_255", [40, 512], F32),
            "q_pl": a("c_q_pl", [40, S], BF16),
            "mj_pl": a("c_mj_pl", [40, S], BF16),
            "ones": a("c_ones", [40, 512], F32),
            "f32": a("c_f32", [40, 512], F32),
            "qf": a("c_qf", [128, 40], F32),
            "of": a("c_of", [128, 40], F32),
            "oi": a("c_oi", [128, 40], I32),
        }
        return cs

    CS = [alloc_cset("_s0", offsets["qTq"]), alloc_cset("_s1", F_END)]




    es = ExitStack()
    SD0 = es.enter_context(nc.psum_tensor("sd0", [128, 1024], F32))
    SD1 = es.enter_context(nc.psum_tensor("sd1", [128, 1024], F32))
    B47 = [es.enter_context(nc.psum_tensor(f"bank{i}", [128, 512], F32)) for i in range(4, 8)]
    B = [SD0.ap()[:, 0:512], SD0.ap()[:, 512:1024],
         SD1.ap()[:, 0:512], SD1.ap()[:, 512:1024],
         B47[0].ap(), B47[1].ap(), B47[2].ap(), B47[3].ap()]
    SDP = [SD0.ap(), SD1.ap()]

    sem_names = [
        "dXa", "dXb",
        "dW",
        "dX1",
        "dX1b",
        "dMt",
        "dAcc",
        "dW1a", "dW1b",
        "dWc",
        "dW2",
        "dGa",
        "dMs",
        "dScA", "dScB",
        "dEx",
        "dTap",
        "dOut",
        "pe", "sc", "ve", "gp",
        "fve", "fsc",
    ]
    sems = {}
    for s_ in sem_names:
        sems[s_] = es.enter_context(nc.semaphore(s_))

    engines = {"sy": nc.sync, "te": nc.tensor, "sc": nc.scalar,
               "ve": nc.vector, "gp": nc.gpsimd}

    def walk(c: Ctx):
        marks = {}

        def mark(key, sem=None, val=None):
            if sem is None:
                marks.pop(key, None)
            else:
                marks[key] = (sem, val if val is not None else c.cnt[sem])

        def wait_mark(eng, key):
            if key in marks:
                sem, val = marks[key]
                c.w(eng, sem, val)

        ALL_SEMS = ["dXa", "dXb", "dW", "dX1", "dX1b", "dMt", "dAcc", "dW1a", "dW1b", "dWc",
                    "dW2", "dGa", "dMs", "dScA", "dScB", "dEx", "dOut", "dTap",
                    "fve", "fsc",
                    "pe", "sc", "ve", "gp"]

        def tap(nm, src_ap, eng_sem, val):
            if nm in dbg_taps:
                c.w("gp", eng_sem, val)
                c.do("gp", lambda e, nm=nm, src_ap=src_ap: e.dma_start(
                    out=dbg_taps[nm], in_=src_ap), inc="dTap", inc_n=16)

        bnd = {}
        if c.cur == "gp":
            for nm, v in (("mt", NE * 2 * KSEL - 1), ("acc", NROW - 1)):
                r = c.eng["gp"].alloc_register(f"bnd_{nm}")
                c.eng["gp"].reg_mov(r, v)
                bnd[nm] = r

        def phase_gate():
            for eng in ENGS:
                for sm in ALL_SEMS:
                    c.w(eng, sm, c.cnt[sm])

        c.do("gp", lambda e: e.dma_start(out=ident_f[:], in_=dident), inc="dW", inc_n=16)
        c.do("gp", lambda e: e.dma_start(out=ident_b[:], in_=dident), inc="dW", inc_n=16)
        c.do("gp", lambda e: e.dma_start(
            out=gate_b[:].rearrange("p (c w) -> p c w", w=8),
            in_=dgate.rearrange("(c p) w -> p c w", p=128)), inc="dW", inc_n=16)
        c.do("gp", lambda e: e.dma_start(out=bqT[:], in_=dbqT), inc="dW", inc_n=16)
        c.do("gp", lambda e: e.dma_start(out=bkT[:], in_=dbkT), inc="dW", inc_n=16)
        c.do("gp", lambda e: e.dma_start(out=bv_row[:], in_=dbv), inc="dW", inc_n=16)
        c.do("gp", lambda e: e.dma_start(out=bo_row[:], in_=dbo), inc="dW", inc_n=16)
        c.do("gp", lambda e: e.dma_start(out=gm_row[:], in_=dgm), inc="dW", inc_n=16)
        c.do("gp", lambda e: e.dma_start(out=bt_row[:], in_=dbt), inc="dW", inc_n=16)
        c.do("gp", lambda e: e.dma_start(out=ne4_sb[:], in_=dne4), inc="dW", inc_n=16)
        init_w = c.cnt["dW"]

        c.w("ve", "dW", init_w)
        c.do("ve", lambda e: e.memset(ones_r[:], 1.0))
        c.do("ve", lambda e: e.memset(ones_rf[:], 1.0))
        c.do("ve", lambda e: e.memset(zero_b[:], 0.0))
        c.do("ve", lambda e: e.memset(probs_en[:], 0.0), inc="ve")
        ve_init = c.cnt["ve"]

        for half in range(2):
            c.w("te", "dW", init_w)
            c.w("te", "ve", ve_init)
            if half == 1:
                c.w("te", "sc", c.cnt["sc"])
            c.do("te", lambda e, h=half: e.matmul(
                B[6][:, 0:512], ones_rf[:], gm_row[:, h * 512:(h + 1) * 512],
                start=True, stop=True), inc="pe")
            c.do("te", lambda e, h=half: e.matmul(
                B[7][:, 0:512], ones_rf[:], bt_row[:, h * 512:(h + 1) * 512],
                start=True, stop=True), inc="pe")
            c.w("sc", "pe", c.cnt["pe"])
            c.do("sc", lambda e, h=half: e.copy(gm_bc[:, h * 512:(h + 1) * 512], B[6][:, 0:512]))
            c.do("sc", lambda e, h=half: e.copy(bt_bc[:, h * 512:(h + 1) * 512], B[7][:, 0:512]),
                 inc="sc")
        bc_done = c.cnt["sc"]
        mark("bank6", "sc", bc_done)
        mark("bank7", "sc", bc_done)

        c.w("gp", "ve", ve_init)
        for r in range(NROW // 128):
            c.do("gp", lambda e, r=r: e.dma_start(
                out=d_acc[r * 128:(r + 1) * 128, :], in_=zero_b[:]), inc="dAcc", inc_n=16)
        c.do("gp", lambda e: e.dma_start(
            out=d_mt.rearrange("(q k) c -> q (k c)", q=128),
            in_=zero_b[:, 0:128]), inc="dMt", inc_n=16)
        mt_prefill = c.cnt["dMt"]

        def gen_stage_c(n):
            cs = CS[n]
            base_p = 0 if n == 0 else 32
            R = probs_en[base_p:base_p + 8, :]
            Rfull = probs_en
            c.w("ve", "sc", c.cnt["sc"])
            c.w("ve", "pe", c.cnt["pe"])
            c.do("ve", lambda e: e.memset(cs["c255"][:], 255.0))
            c.do("ve", lambda e: e.memset(small[:, 0:1], 0.0))
            c.do("ve", lambda e: e.memset(small[:, 1:2], 1.0), inc="ve", flush=True)
            lo, hi = 0, 1
            for it in range(48):
                c.do("ve", lambda e, lo=lo, hi=hi: e.tensor_scalar(
                    small[:, 2:3], small[:, lo:lo + 1], small[:, hi:hi + 1],
                    0.5, OP.add, OP.mult), flush=True)
                c.do("ve", lambda e: e.tensor_scalar(
                    cs["dummy"][:], Rfull[:], small[:, 2:3], None, OP.is_ge, OP.add,
                    accum_out=small[:, 3:4]), flush=True)
                c.do("ve", lambda e: e.tensor_scalar(
                    smalli[:, 0:1], small[:, 3:4], 511.5, None, OP.is_ge), flush=True)
                nlo, nhi = (4, 5) if lo == 0 else (0, 1)
                c.do("ve", lambda e, lo=lo, nlo=nlo: e.select(
                    small[:, nlo:nlo + 1], smalli[:, 0:1], small[:, 2:3],
                    small[:, lo:lo + 1]))
                c.do("ve", lambda e, hi=hi, nhi=nhi: e.select(
                    small[:, nhi:nhi + 1], smalli[:, 0:1], small[:, hi:hi + 1],
                    small[:, 2:3]), inc="ve", flush=True)
                lo, hi = nlo, nhi
                yield
            c.do("ve", lambda e, hi=hi: e.tensor_scalar(
                cs["dummy"][:], Rfull[:], small[:, hi:hi + 1], None, OP.is_ge, OP.add,
                accum_out=small[:, 6:7]), flush=True)
            c.do("ve", lambda e: e.tensor_scalar(
                small[:, 7:8], small[:, 6:7], -1.0, None, OP.add), flush=True)
            c.do("ve", lambda e, hi=hi: e.tensor_scalar(
                cs["A"][:], Rfull[:], small[:, hi:hi + 1], None, OP.is_ge))
            c.do("ve", lambda e, lo=lo: e.tensor_scalar(
                cs["T"][:], Rfull[:], small[:, lo:lo + 1], None, OP.is_ge))
            c.do("ve", lambda e: e.tensor_sub(cs["T"][:], cs["T"][:], cs["A"][:]), inc="ve")
            c.do("ve", lambda e: e.memset(cs["ones"][:], 1.0))
            c.do("ve", lambda e: e.memset(small[:, 8:9], 0.0))
            c.do("ve", lambda e: e.memset(small[:, 9:10], 0.0), inc="ve")
            for ch in range(4):
                sl = slice(ch * 512, (ch + 1) * 512)
                c.do("ve", lambda e, sl=sl: e.tensor_scalar(
                    cs["f32"][:], cs["A"][:, sl], 0.0, None, OP.add))
                c.do("ve", lambda e: e.tensor_tensor_scan(
                    cs["csS"][:], cs["f32"][:], cs["ones"][:], small[:, 8:9], OP.add, OP.mult))
                c.do("ve", lambda e, sl=sl: e.tensor_scalar(
                    cs["f32"][:], cs["T"][:, sl], 0.0, None, OP.add))
                c.do("ve", lambda e: e.tensor_tensor_scan(
                    cs["csT"][:], cs["f32"][:], cs["ones"][:], small[:, 9:10], OP.add, OP.mult))
                c.do("ve", lambda e: e.tensor_copy(small[:, 8:9], cs["csS"][:, 511:512]))
                c.do("ve", lambda e: e.tensor_copy(small[:, 9:10], cs["csT"][:, 511:512]))
                c.do("ve", lambda e: e.tensor_scalar(
                    cs["csS"][:], cs["csS"][:], -1.0, None, OP.add))
                c.do("ve", lambda e: e.tensor_scalar(
                    cs["csT"][:], cs["csT"][:], small[:, 7:8], None, OP.add))
                c.do("ve", lambda e, sl=sl: e.tensor_scalar(
                    cs["keep"][:], cs["A"][:, sl], 0.5, None, OP.is_ge))
                c.do("ve", lambda e: e.select(cs["r"][:], cs["keep"][:], cs["csS"][:], cs["csT"][:]))
                c.do("ve", lambda e, sl=sl: e.tensor_add(cs["k1"][:], cs["A"][:, sl], cs["T"][:, sl]))
                c.do("ve", lambda e: e.tensor_scalar(
                    cs["csT"][:], cs["r"][:], 511.5, None, OP.is_le))
                c.do("ve", lambda e: e.tensor_scalar(
                    cs["csS"][:], cs["k1"][:], 0.0, None, OP.add))
                c.do("ve", lambda e: e.tensor_mul(cs["csS"][:], cs["csS"][:], cs["csT"][:]))
                c.do("ve", lambda e: e.tensor_scalar(
                    cs["keep"][:], cs["csS"][:], 0.5, None, OP.is_ge))
                c.do("ve", lambda e: e.tensor_scalar(cs["j"][:], cs["r"][:], 128.0, None, OP.is_ge))
                c.do("ve", lambda e: e.tensor_scalar(cs["csT"][:], cs["r"][:], 256.0, None, OP.is_ge))
                c.do("ve", lambda e: e.tensor_add(cs["j"][:], cs["j"][:], cs["csT"][:]))
                c.do("ve", lambda e: e.tensor_scalar(cs["csT"][:], cs["r"][:], 384.0, None, OP.is_ge))
                c.do("ve", lambda e: e.tensor_add(cs["j"][:], cs["j"][:], cs["csT"][:]))
                c.do("ve", lambda e: e.scalar_tensor_tensor(
                    cs["csT"][:], cs["j"][:], -128.0, cs["r"][:], OP.mult, OP.add))
                c.do("ve", lambda e, sl=sl: e.select(
                    cs["q_pl"][:, sl], cs["keep"][:], cs["csT"][:], cs["c255"][:]))
                c.do("ve", lambda e, sl=sl: e.tensor_scalar(
                    cs["mj_pl"][:, sl], cs["j"][:], ne4_sb[:], None, OP.add), inc="ve")
                yield
            plane_m = c.cnt["ve"]

            for tt in range(TT):
                tok0 = tt * 128
                c.w("te", "ve", plane_m)
                wait_mark("te", "bank7")
                c.do("te", lambda e, tok0=tok0: e.matmul(
                    B[7][0:128, 0:40], cs["q_pl"][0:40, tok0:tok0 + 128],
                    ident_b[0:40, 0:40], start=True, stop=True))
                c.do("te", lambda e, tok0=tok0: e.matmul(
                    B[7][0:128, 64:104], cs["mj_pl"][0:40, tok0:tok0 + 128],
                    ident_b[0:40, 0:40], start=True, stop=True), inc="pe")
                trp_m = c.cnt["pe"]
                c.w("sc", "pe", trp_m)
                wait_mark("sc", "c_qf")
                c.do("sc", lambda e: e.copy(cs["qf"][:, 0:40], B[7][:, 0:40]), inc="sc")
                qf_m = c.cnt["sc"]
                c.w("ve", "sc", qf_m)
                c.w("ve", "pe", trp_m)
                wait_mark("ve", "c_oi")
                c.do("ve", lambda e: e.scalar_tensor_tensor(
                    cs["of"][:, 0:40], cs["qf"][:, 0:40], 64.0, B[7][:, 64:104],
                    OP.mult, OP.add), flush=True)
                c.do("ve", lambda e: e.tensor_scalar(
                    cs["oi"][:, 0:40], cs["of"][:, 0:40], 0.0, None, OP.add), inc="ve")
                oi_m = c.cnt["ve"]
                mark("bank7", "ve", oi_m)
                mark("c_qf", "ve", oi_m)

                c.w("gp", "ve", ve_init)
                wait_mark("gp", "gsrci")
                c.do("gp", lambda e, n=n, tok0=tok0: e.iota(
                    gsrci[:], pattern=[[0, 8]], base=n * S + tok0,
                    channel_multiplier=1), inc="gp")
                io_m = c.cnt["gp"]
                c.w("ve", "gp", io_m)
                wait_mark("ve", "gsrc")
                c.do("ve", lambda e: e.tensor_scalar(
                    gsrc[:].rearrange("p (k c) -> p k c", c=2)[:, :, 0:1],
                    gsrci[:], 0.0, None, OP.add))
                ptcol = (n * TT + tt) * 8
                c.do("ve", lambda e, ptcol=ptcol: e.tensor_copy(
                    gsrc[:].rearrange("p (k c) -> p k c", c=2)[:, :, 1:2],
                    probs_tok[:, ptcol:ptcol + 8]), inc="ve")
                gsrc_m = c.cnt["ve"]
                mark("gsrci", "ve", gsrc_m)

                c.w("gp", "ve", gsrc_m)
                c.w("gp", "ve", oi_m)
                c.w("gp", "dMt", mt_prefill)
                for e_ in range(8):
                    colx = n * 32 + e_
                    c.do("gp", lambda e, e_=e_, colx=colx: e.indirect_dma_start(
                        out=d_mt,
                        out_offset=IndirectOffsetOnAxis(cs["oi"][:, colx:colx + 1], 0),
                        in_=gsrc[:, 2 * e_:2 * e_ + 2], in_offset=None,
                        bounds_check=bnd["mt"], oob_is_err=False),
                        inc="dMt", inc_n=16)
                mark("gsrc", "dMt", c.cnt["dMt"])
                mark("c_oi", "dMt", c.cnt["dMt"])
                yield

        pending_c = {}
        c_done = {}
        hoist_state = {}
        wc = {}

        def drive_c(n, k):
            g = pending_c.get(n)
            if g is None or c_done.get(n):
                return
            for _ in range(k):
                try:
                    next(g)
                except StopIteration:
                    c_done[n] = True
                    return

        def drain_c(n):
            g = pending_c.get(n)
            if g is None or c_done.get(n):
                return
            while True:
                try:
                    next(g)
                except StopIteration:
                    c_done[n] = True
                    return

        for s in range(NSEQ):
            phase_gate()
            wait_mark("gp", "wsl")
            for slot, dW_ in ((wsl0, dWk), (wsl1, dWv)):
                c.do("gp", lambda e, sl=slot, dd=dW_: e.dma_start(
                    out=sl[:].rearrange("p (c w) -> p c w", w=D),
                    in_=dd.rearrange("(c p) w -> p c w", p=128)), inc="dW", inc_n=16)
            wkv_loaded = c.cnt["dW"]

            if f"A{s}" not in _inc:
                break
            c.do("ve", lambda e: e.memset(
                vaug[:].rearrange("p (t h w) -> p t h w", h=H, w=65)[:, :, :, 64:65],
                1.0), inc="ve")
            vaug_ones = c.cnt["ve"]
            for q4 in range(4):
                xq = xTq[q4 % 2]
                wait_mark("gp", f"xTq{q4 % 2}")
                c.do("gp", lambda e, s=s, q4=q4, xq=xq: e.dma_start(
                    out=xq[:].rearrange("p (k w) -> p k w", w=512),
                    in_=dxt[s].rearrange("(k p) t -> p k t", p=128)
                    [:, :, q4 * 512:(q4 + 1) * 512]), inc="dXa", inc_n=16)
                xq_ld = c.cnt["dXa"]
                c.w("te", "dXa", xq_ld)
                c.w("te", "dW", wkv_loaded)
                for cc in range(8):
                    bk = B[2 + (cc % 2)]
                    wait_mark("te", f"bank{2 + (cc % 2)}")
                    for dinc in range(8):
                        c.do("te", lambda e, cc=cc, dinc=dinc, bk=bk, xq=xq: e.matmul(
                            bk[:],
                            wsl0[:, dinc * D + cc * 128: dinc * D + cc * 128 + 128],
                            xq[:, dinc * 512: dinc * 512 + 512],
                            start=(dinc == 0), stop=(dinc == 7)),
                            inc="pe" if dinc == 7 else None)
                    kp_m = c.cnt["pe"]
                    c.w("sc", "pe", kp_m)
                    c.do("sc", lambda e, cc=cc, bk=bk, q4=q4: e.activation(
                        kT[:, cc * S + q4 * 512: cc * S + q4 * 512 + 512],
                        bk[:], AF.Identity, bias=bkT[:, cc:cc + 1]), inc="sc")
                    mark(f"bank{2 + (cc % 2)}", "sc", c.cnt["sc"])
                for j in range(4):
                    tt = q4 * 4 + j
                    c.w("te", "ve", ve_init)
                    for dh in range(2):
                        bk = B[4 + dh]
                        wait_mark("te", f"bank{4 + dh}")
                        for dinc in range(8):
                            c.do("te", lambda e, dh=dh, dinc=dinc, bk=bk, xq=xq, j=j: e.matmul(
                                bk[:],
                                xq[:, dinc * 512 + j * 128: dinc * 512 + j * 128 + 128],
                                wsl1[:, dinc * D + dh * 512: dinc * D + dh * 512 + 512],
                                start=(dinc == 0), stop=False))
                        c.do("te", lambda e, dh=dh, bk=bk: e.matmul(
                            bk[:], ones_r[:], bv_row[:, dh * 512:(dh + 1) * 512],
                            start=False, stop=True),
                            inc="pe" if dh == 1 else None)
                    vproj_mark = c.cnt["pe"]
                    c.w("sc", "pe", vproj_mark)
                    for dh in range(2):
                        bk = B[4 + dh]
                        base = tt * (H * 65) + dh * 8 * 65
                        c.do("sc", lambda e, bk=bk, base=base: e.copy(
                            vaug[:, base:base + 8 * 65].rearrange(
                                "p (h w) -> p h w", w=65)[:, :, 0:64],
                            bk[:].rearrange("p (h w) -> p h w", w=64)),
                            inc="sc")
                        mark(f"bank{4 + dh}", "sc", c.cnt["sc"])
                mark(f"xTq{q4 % 2}", "pe", c.cnt["pe"])
                if s == 1:
                    drive_c(0, 18)
            if s == 1:
                drain_c(0)
                for eng in ("te", "sc", "ve", "gp", "sy"):
                    c.w(eng, "dMt", c.cnt["dMt"])
                    c.w(eng, "ve", c.cnt["ve"])
                    c.w(eng, "pe", c.cnt["pe"])

            a_done_pe = c.cnt["pe"]
            a_done_sc = c.cnt["sc"]
            if s == 0:
                tap("dbg_kT", kT[:], "sc", a_done_sc)
                tap("dbg_vaug", vaug[:], "sc", a_done_sc)

            c.w("gp", "pe", a_done_pe)
            for slot, dW_ in ((wsl0, dWq), (wsl1, dWo)):
                c.do("gp", lambda e, sl=slot, dd=dW_: e.dma_start(
                    out=sl[:].rearrange("p (c w) -> p c w", w=D),
                    in_=dd.rearrange("(c p) w -> p c w", p=128)), inc="dW", inc_n=16)
            wqo_loaded = c.cnt["dW"]

            if f"B{s}" not in _inc:
                break
            c.w("te", "ve", vaug_ones)
            for qt in range(4):
                xq = xTq[qt % 2]
                wait_mark("gp", f"xTq{qt % 2}")
                c.do("gp", lambda e, s=s, qt=qt, xq=xq: e.dma_start(
                    out=xq[:].rearrange("p (k w) -> p k w", w=512),
                    in_=dxt[s].rearrange("(k p) t -> p k t", p=128)
                    [:, :, qt * 512:(qt + 1) * 512]), inc="dXa", inc_n=16)
                xq_ld = c.cnt["dXa"]
                c.w("te", "dXa", xq_ld)
                c.w("te", "dW", wqo_loaded)
                for cc in range(8):
                    bk = B[2 + (cc % 2)]
                    wait_mark("te", f"bank{2 + (cc % 2)}")
                    for dinc in range(8):
                        c.do("te", lambda e, cc=cc, dinc=dinc, bk=bk, xq=xq: e.matmul(
                            bk[:],
                            wsl0[:, dinc * D + cc * 128: dinc * D + cc * 128 + 128],
                            xq[:, dinc * 512: dinc * 512 + 512],
                            start=(dinc == 0), stop=(dinc == 7)),
                            inc="pe" if dinc == 7 else None)
                    qp_m = c.cnt["pe"]
                    c.w("sc", "pe", qp_m)
                    c.do("sc", lambda e, cc=cc, bk=bk: e.activation(
                        qTq[:, cc * 512: cc * 512 + 512],
                        bk[:], AF.Identity, bias=bqT[:, cc:cc + 1]), inc="sc")
                    mark(f"bank{2 + (cc % 2)}", "sc", c.cnt["sc"])
                mark(f"xTq{qt % 2}", "pe", c.cnt["pe"])
                qT_mark = c.cnt["sc"]
                if s == 0 and qt == 0:
                    tap("dbg_qT", qTq[:], "sc", qT_mark)

                def emit_scores(hp, kc):
                    cchunk = hp
                    pair = 0 if kc % 2 == 0 else 1
                    sb0 = B[0] if pair == 0 else B[2]
                    sb1 = B[1] if pair == 0 else B[3]
                    c.w("te", "sc", qT_mark)
                    wait_mark("te", f"bank{0 if pair == 0 else 2}")
                    wait_mark("te", f"bank{1 if pair == 0 else 3}")
                    c.do("te", lambda e, sb0=sb0, cchunk=cchunk, kc=kc: e.matmul(
                        sb0[:],
                        kT[0:64, cchunk * S + kc * 128: cchunk * S + kc * 128 + 128],
                        qTq[0:64, cchunk * 512: cchunk * 512 + 512],
                        start=True, stop=True), inc="pe")
                    c.do("te", lambda e, sb1=sb1, cchunk=cchunk, kc=kc: e.matmul(
                        sb1[:],
                        kT[64:128, cchunk * S + kc * 128: cchunk * S + kc * 128 + 128],
                        qTq[64:128, cchunk * 512: cchunk * 512 + 512],
                        start=True, stop=True), inc="pe")
                    sc_mark = c.cnt["pe"]
                    c.w("sc", "pe", sc_mark)
                    wait_mark("sc", f"pT{0 if pair == 0 else 2}")
                    wait_mark("sc", f"pT{1 if pair == 0 else 3}")
                    c.do("sc", lambda e, pair=pair: e.activation(
                        pTT[:, pair * 1024:(pair + 1) * 1024], SDP[pair][:],
                        AF.Exp, scale=0.125), inc="sc")
                    exp_mark = c.cnt["sc"]
                    mark(f"bank{0 if pair == 0 else 2}", "sc", exp_mark)
                    mark(f"bank{1 if pair == 0 else 3}", "sc", exp_mark)
                    return exp_mark

                def emit_av(hp, kc, exp_mark):
                    pair = 0 if kc % 2 == 0 else 1
                    c.w("te", "sc", exp_mark)
                    for hi in range(2):
                        h = 2 * hp + hi
                        vbase = kc * (H * 65) + h * 65
                        pp = pTT[:, pair * 1024 + hi * 512: pair * 1024 + hi * 512 + 512]
                        ob = B[4 + hi]
                        c.do("te", lambda e, pp=pp, ob=ob, vbase=vbase, kc=kc: e.matmul(
                            ob[0:65, :], vaug[:, vbase:vbase + 65], pp,
                            start=(kc == 0), stop=(kc == TT - 1)),
                            inc="pe" if (kc == TT - 1 and hi == 1) else None)
                    if kc == TT - 1:
                        av_mark = c.cnt["pe"]
                        mark(f"pT{2 * pair}", "pe", av_mark)
                        mark(f"pT{2 * pair + 1}", "pe", av_mark)
                        c.w("ve", "pe", av_mark)
                        par = hp % 2
                        wait_mark("ve", f"o_full{par}")
                        c.do("ve", lambda e, par=par: e.tensor_copy(
                            o_full[par][0:65, 0:512], B[4][0:65, :]))
                        c.do("ve", lambda e, par=par: e.tensor_copy(
                            o_full[par][0:65, 512:1024], B[5][0:65, :]), inc="ve")
                        oc_mark = c.cnt["ve"]
                        mark("bank4", "ve", oc_mark)
                        mark("bank5", "ve", oc_mark)
                        return oc_mark
                    else:
                        av_part = c.cnt["pe"]
                        mark(f"pT{2 * pair}", "pe", av_part)
                        mark(f"pT{2 * pair + 1}", "pe", av_part)
                        return None

                def emit_normalize(hp, oc_mark):
                    par = hp % 2
                    cchunk = hp
                    c.w("ve", "ve", oc_mark)
                    for hi in range(2):
                        h = 2 * hp + hi
                        rg = 64 * (h % 2)
                        sl = slice(hi * 512, hi * 512 + 512)
                        c.do("ve", lambda e, par=par, sl=sl: e.reciprocal(
                            epi_t1[0:1, sl], o_full[par][64:65, sl]),
                            inc="ve", flush=True)
                        rec_m = c.cnt["ve"]
                        bb = B[6] if hi == 0 else B[7]
                        c.w("te", "ve", rec_m)
                        wait_mark("te", f"bank{6 + hi}")
                        c.do("te", lambda e, bb=bb, sl=sl: e.matmul(
                            bb[0:64, :], ones_rf[:, 0:64], epi_t1[0:1, sl],
                            start=True, stop=True), inc="pe")
                        bc_m = c.cnt["pe"]
                        c.w("ve", "pe", bc_m)
                        c.do("ve", lambda e, rg=rg, cchunk=cchunk, par=par, sl=sl, bb=bb: e.tensor_mul(
                            oT[rg:rg + 64, cchunk * 512:(cchunk + 1) * 512],
                            o_full[par][0:64, sl], bb[0:64, :]), inc="ve")
                        mark(f"bank{6 + hi}", "ve", c.cnt["ve"])
                    mark(f"o_full{par}", "ve", c.cnt["ve"])

                pend_norm = None
                for hp in range(8):
                    exp_marks = {}
                    for kc in range(TT):
                        exp_marks[kc] = emit_scores(hp, kc)
                        if kc == 1 and pend_norm is not None:
                            emit_normalize(*pend_norm)
                            pend_norm = None
                        if kc >= 1:
                            emit_av(hp, kc - 1, exp_marks[kc - 1])
                    oc = emit_av(hp, TT - 1, exp_marks[TT - 1])
                    pend_norm = (hp, oc)
                emit_normalize(*pend_norm)
                pend_norm = None
                oT_mark = c.cnt["ve"]
                if s == 0 and qt == 0:
                    tap("dbg_oT", oT[:], "ve", oT_mark)

                for j in range(4):
                    tt = qt * 4 + j
                    row0 = s * S + tt * 128
                    c.w("te", "ve", oT_mark)
                    c.w("te", "dW", wqo_loaded)
                    wait_mark("te", "bank0")
                    wait_mark("te", "bank1")
                    for dh in range(2):
                        bk = B[dh]
                        for cc in range(8):
                            c.do("te", lambda e, dh=dh, cc=cc, bk=bk, j=j: e.matmul(
                                bk[:],
                                oT[:, cc * 512 + j * 128: cc * 512 + j * 128 + 128],
                                wsl1[:, cc * D + dh * 512: cc * D + dh * 512 + 512],
                                start=(cc == 0), stop=False))
                        c.do("te", lambda e, dh=dh, bk=bk: e.matmul(
                            bk[:], ones_r[:], bo_row[:, dh * 512:(dh + 1) * 512],
                            start=False, stop=True),
                            inc="pe" if dh == 1 else None)
                    oproj_mark = c.cnt["pe"]

                    xe = xe_f[j % 2]
                    wait_mark("sy", f"xe{j % 2}")
                    c.do("sy", lambda e, s=s, tt=tt, xe=xe: e.dma_start(
                        out=xe[:], in_=dx[s, tt * 128:(tt + 1) * 128, :]),
                        inc="dXb", inc_n=16)
                    xe_ld = c.cnt["dXb"]

                    c.w("ve", "pe", oproj_mark)
                    c.w("ve", "dXb", xe_ld)
                    wait_mark("ve", "epi")
                    for dh in range(2):
                        c.do("ve", lambda e, dh=dh, xe=xe: e.tensor_add(
                            epi_z[:, dh * 512:(dh + 1) * 512], B[dh][:],
                            xe[:, dh * 512:(dh + 1) * 512]),
                            inc="ve" if dh == 1 else None)
                    z_mark = c.cnt["ve"]
                    mark("bank0", "ve", z_mark)
                    mark("bank1", "ve", z_mark)
                    if s == 0 and qt == 0 and j == 0:
                        tap("dbg_z", epi_z[:], "ve", z_mark)

                    def emit_ln(src, dst, eps, col, affine):
                        entry_ve = c.cnt["ve"]
                        c.w("ve", "sc", c.cnt["sc"])
                        c.do("ve", lambda e: e.tensor_reduce(
                            stats[:, col:col + 1], src[:], mybir.AxisListType.X,
                            OP.add), flush=True)
                        c.w("sc", "ve", entry_ve)
                        c.do("sc", lambda e: e.activation(
                            epi_t1[:], src[:], AF.Square,
                            accum_out=stats[:, col + 1:col + 2]), inc="sc")
                        sq_m = c.cnt["sc"]
                        c.do("ve", lambda e: e.tensor_scalar(
                            stats[:, col + 2:col + 3], stats[:, col:col + 1],
                            1.0 / D, None, OP.mult), flush=True)
                        c.do("ve", lambda e: e.tensor_scalar(
                            stats[:, col + 3:col + 4], stats[:, col + 2:col + 3],
                            stats[:, col + 2:col + 3], None, OP.mult), flush=True)
                        c.w("ve", "sc", sq_m)
                        c.do("ve", lambda e: e.tensor_scalar(
                            stats[:, col + 1:col + 2], stats[:, col + 1:col + 2],
                            1.0 / D, stats[:, col + 3:col + 4], OP.mult, OP.subtract),
                            flush=True)
                        c.do("ve", lambda e: e.tensor_scalar(
                            stats[:, col + 1:col + 2], stats[:, col + 1:col + 2],
                            eps, None, OP.add), inc="ve", flush=True)
                        v_m = c.cnt["ve"]
                        c.w("sc", "ve", v_m)
                        c.do("sc", lambda e: e.activation(
                            stats[:, col + 1:col + 2], stats[:, col + 1:col + 2],
                            AF.Ln), flush=True)
                        c.do("sc", lambda e: e.activation(
                            stats[:, col + 1:col + 2], stats[:, col + 1:col + 2],
                            AF.Exp, scale=-0.5), inc="sc")
                        rstd_m = c.cnt["sc"]
                        c.w("ve", "sc", rstd_m)
                        if affine:
                            c.do("ve", lambda e: e.tensor_scalar(
                                epi_t1[:], src[:], stats[:, col + 2:col + 3],
                                stats[:, col + 1:col + 2], OP.subtract, OP.mult))
                            c.do("ve", lambda e: e.tensor_mul(epi_t1[:], epi_t1[:], gm_bc[:]))
                            c.do("ve", lambda e: e.tensor_add(dst[:], epi_t1[:], bt_bc[:]),
                                 inc="ve")
                        else:
                            c.do("ve", lambda e: e.tensor_scalar(
                                dst[:], src[:], stats[:, col + 2:col + 3],
                                stats[:, col + 1:col + 2], OP.subtract, OP.mult),
                                inc="ve")
                        return c.cnt["ve"]

                    if dbg_store == "z":
                        c.do("ve", lambda e: e.tensor_copy(x1f_st[2][:], epi_z[:]), inc="ve")
                    emit_ln(epi_z, epi_z, 1e-6, 0, affine=True)
                    if dbg_store == "abo":
                        c.do("ve", lambda e: e.tensor_copy(x1f_st[2][:], epi_z[:]), inc="ve")
                    c.do("ve", lambda e, xe=xe: e.tensor_add(epi_z[:], epi_z[:], xe[:]),
                         inc="ve")
                    if dbg_store == "u":
                        c.do("ve", lambda e: e.tensor_copy(x1f_st[2][:], epi_z[:]), inc="ve")
                    if dbg_store is not None:
                        dsm = c.cnt["ve"]
                        c.w("sy", "ve", dsm)
                        wait_mark("sy", "x1dbg")
                        c.do("sy", lambda e, row0=row0: e.dma_start(
                            out=d_x1f[row0:row0 + 128, :], in_=x1f_st[2][:]),
                            inc="dX1", inc_n=16)
                        mark("x1dbg", "dX1", c.cnt["dX1"])
                    mark(f"xe{j % 2}", "ve", c.cnt["ve"])
                    if s == 0 and qt == 0 and j == 0:
                        tap("dbg_u", epi_z[:], "ve", c.cnt["ve"])
                    x1buf = x1f_st[j % 2]
                    wait_mark("ve", f"x1f{j % 2}")
                    x1_m = emit_ln(epi_z, x1buf, 1e-5, 4, affine=False)
                    mark("epi", "ve", x1_m)
                    if s == 0 and qt == 0 and j == 0:
                        tap("dbg_x1t", x1buf[:], "ve", x1_m)
                        tap("dbg_stats", stats[:], "ve", x1_m)
                    c.w("sc", "ve", x1_m)
                    wait_mark("sc", f"x1b{j % 2}")
                    x1b = x1b_st[j % 2]
                    c.do("sc", lambda e, x1b=x1b, x1buf=x1buf: e.copy(x1b[:], x1buf[:]),
                         inc="sc")
                    x1b_m = c.cnt["sc"]
                    c.w("sy", "ve", x1_m)
                    if dbg_store is None:
                        c.do("sy", lambda e, x1buf=x1buf, row0=row0: e.dma_start(
                            out=d_x1f[row0:row0 + 128, :], in_=x1buf[:]),
                            inc="dX1", inc_n=16)
                    c.w("gp", "sc", x1b_m)
                    c.do("gp", lambda e, x1b=x1b, row0=row0: e.dma_start(
                        out=d_x1b[row0:row0 + 128, :], in_=x1b[:]), inc="dX1b", inc_n=16)
                    mark(f"x1f{j % 2}", "dX1", c.cnt["dX1"])
                    mark(f"x1b{j % 2}", "dX1b", c.cnt["dX1b"])

                    c.w("te", "sc", x1b_m)
                    wait_mark("te", "bank2")
                    wait_mark("te", "bank3")
                    for jj in range(8):
                        bk = B[2] if jj < 4 else B[3]
                        col = (jj % 4) * 128
                        c.do("te", lambda e, jj=jj, bk=bk, col=col, x1b=x1b: e.matmul(
                            bk[:, col:col + 128], x1b[:, jj * 128:(jj + 1) * 128],
                            ident_b[:], start=True, stop=True),
                            inc="pe" if jj == 7 else None)
                    gtp_mark = c.cnt["pe"]
                    c.w("ve", "pe", gtp_mark)
                    wait_mark("ve", "x1T_t")
                    c.do("ve", lambda e: e.tensor_copy(x1T_t[:, 0:512], B[2][:]))
                    c.do("ve", lambda e: e.tensor_copy(x1T_t[:, 512:1024], B[3][:]), inc="ve")
                    g_xT = c.cnt["ve"]
                    mark("bank2", "ve", g_xT)
                    mark("bank3", "ve", g_xT)
                    c.w("te", "ve", g_xT)
                    wait_mark("te", "bank6")
                    for dinc in range(8):
                        c.do("te", lambda e, dinc=dinc: e.matmul(
                            B[6][:, 0:8], x1T_t[:, dinc * 128:(dinc + 1) * 128],
                            gate_b[:, dinc * 8:(dinc + 1) * 8],
                            start=(dinc == 0), stop=(dinc == 7)),
                            inc="pe" if dinc == 7 else None)
                    mark("x1T_t", "pe", c.cnt["pe"])
                    gsc_mark = c.cnt["pe"]
                    c.w("sc", "pe", gsc_mark)
                    c.w("sc", "ve", c.cnt["ve"])
                    c.do("sc", lambda e: e.activation(
                        stats[:, 8:16], B[6][:, 0:8], AF.Exp,
                        accum_out=stats[:, 16:17]), inc="sc")
                    ge_m = c.cnt["sc"]
                    mark("bank6", "sc", ge_m)
                    c.w("ve", "sc", ge_m)
                    c.do("ve", lambda e: e.reciprocal(stats[:, 17:18], stats[:, 16:17]),
                         flush=True)
                    ptcol = (s * TT + tt) * 8
                    c.do("ve", lambda e, ptcol=ptcol: e.tensor_scalar(
                        probs_tok[:, ptcol:ptcol + 8], stats[:, 8:16],
                        stats[:, 17:18], None, OP.mult), inc="ve")
                    pt_m = c.cnt["ve"]
                    c.w("te", "ve", pt_m)
                    wait_mark("te", "bank7")
                    c.do("te", lambda e, ptcol=ptcol: e.matmul(
                        B[7][0:8, 0:128], probs_tok[:, ptcol:ptcol + 8],
                        ident_f[:], start=True, stop=True), inc="pe")
                    pe_tr = c.cnt["pe"]
                    c.w("sc", "pe", pe_tr)
                    base_p = 0 if s == 0 else 32
                    c.do("sc", lambda e, base_p=base_p, tt=tt: e.copy(
                        probs_en[base_p:base_p + 8, tt * 128:(tt + 1) * 128],
                        B[7][0:8, 0:128]), inc="sc")
                    mark("bank7", "sc", c.cnt["sc"])

            b_done = {k: c.cnt[k] for k in ("pe", "sc", "ve")}
            mark("wsl", "pe", b_done["pe"])

            if f"C{s}" in _inc:
                pending_c[s] = gen_stage_c(s)
                if s == NSEQ - 1:
                    pass
            if s == 0:
                hoist_state["x1b0"] = c.cnt["dX1b"]
            if s == 1:
                drain_c(0)
                if "D" in _inc:
                    c.w("gp", "dMt", c.cnt["dMt"])
                    c.do("gp", lambda e: e.dma_start(
                        out=m_sb[:, 0:64],
                        in_=d_mt.rearrange("(q k) c -> q k c", q=128)[:, 0:32, :]),
                        inc="dMs", inc_n=16)
                    c.w("ve", "dMs", c.cnt["dMs"])
                    c.do("ve", lambda e: e.tensor_scalar(
                        m_i32[:, 0:32],
                        m_sb[:, 0:64].rearrange("p (k c) -> p k c", c=2)[:, :, 0],
                        0.0, None, OP.add), inc="ve")
                    hoist_mi0 = c.cnt["ve"]
                    c.w("gp", "ve", hoist_mi0)
                    c.w("gp", "dX1b", hoist_state["x1b0"])
                    for st in range(4):
                        colx = st
                        c.do("gp", lambda e, colx=colx, st=st: e.indirect_dma_start(
                            out=xin4[:, st * D:(st + 1) * D], out_offset=None,
                            in_=d_x1b,
                            in_offset=IndirectOffsetOnAxis(m_i32[:, colx:colx + 1], 0)),
                            inc="dGa", inc_n=16)
                    hoist_state.update(mi0=hoist_mi0, ga=c.cnt["dGa"])
                    for ce in (0, 1):
                        for hh in range(2):
                            for cg in range(hh * 4, hh * 4 + 4):
                                c.do("gp", lambda e, ce=ce, cg=cg: e.dma_start(
                                    out=d_w1b[ce].rearrange(
                                        "p (g k f) -> p g k f", g=8, k=8)[:, cg],
                                    in_=dw1[ce].rearrange(
                                        "(k p) (g f) -> p g k f", p=128, f=512)
                                    [:, cg]),
                                    inc="dWc", inc_n=16)
                        wc[ce] = c.cnt["dWc"]

        mt_n0_done = None
        attn_done = {k: c.cnt[k] for k in ("pe", "sc", "ve")}

        if "D" not in _inc:
            phase_gate()
            return c.cnt
        phase_gate()
        mi_m = [hoist_state["mi0"], None]
        ga_pending = hoist_state["ga"]
        xin4_free = None

        def emit_gathers(ge, gn):
            c.w("gp", "ve", mi_m[gn])
            c.w("gp", "dX1b", c.cnt["dX1b"])
            if xin4_free is not None:
                c.w("gp", *xin4_free)
            for st in range(4):
                colx = (gn * 8 + ge) * 4 + st
                c.do("gp", lambda e, colx=colx, st=st: e.indirect_dma_start(
                    out=xin4[:, st * D:(st + 1) * D], out_offset=None,
                    in_=d_x1b,
                    in_offset=IndirectOffsetOnAxis(m_i32[:, colx:colx + 1], 0)),
                    inc="dGa", inc_n=16)
            return c.cnt["dGa"]

        x1b_all = c.cnt["dX1b"]

        hT_free = None
        w1_slot_free = [None, None]
        cast_iter = iter([(ce, hh) for ce in range(2, NE) for hh in (0, 1)])
        pref_next = False
        w2_free = None
        xinT_free = None
        scat_chain = {0: 0, 1: 0}

        for e_ in range(NE):
            if w2_free is not None:
                c.w("gp", *w2_free)
            c.w("gp", "pe", attn_done["pe"])
            for g in range(8):
                c.do("gp", lambda e, e_=e_, g=g: e.dma_start(
                    out=w2b[:].rearrange("p (f d) -> p f d", d=D)[:, g * 4:(g + 1) * 4, :],
                    in_=dw2[e_].rearrange("(f p) d -> p f d", p=128)[:, g * 4:(g + 1) * 4, :]),
                    inc="dW2", inc_n=16)
            w2_ld = c.cnt["dW2"]

            for n in range(2):
                nst4 = [(n * 8 + e_) * 4 + st for st in range(4)]
                if n == 1 and mi_m[1] is None:
                    drain_c(1)
                    c.w("gp", "dMt", c.cnt["dMt"])
                    c.do("gp", lambda e: e.dma_start(
                        out=m_sb[:, 64:128],
                        in_=d_mt.rearrange("(q k) c -> q k c", q=128)[:, 32:64, :]),
                        inc="dMs", inc_n=16)
                    c.w("ve", "dMs", c.cnt["dMs"])
                    c.do("ve", lambda e: e.tensor_scalar(
                        m_i32[:, 32:64],
                        m_sb[:, 64:128].rearrange("p (k c) -> p k c", c=2)[:, :, 0],
                        0.0, None, OP.add), inc="ve")
                    mi_m[1] = c.cnt["ve"]
                    ga_pending = emit_gathers(e_, n)
                if xinT_free is not None:
                    c.w("te", *xinT_free)
                c.w("te", "dGa", ga_pending)
                c.w("te", "pe", attn_done["pe"])
                c.w("te", "dW", init_w)
                for st in range(4):
                    wait_mark("te", "bank0")
                    wait_mark("te", "bank1")
                    for jj in range(8):
                        bk = B[0] if jj < 4 else B[1]
                        col = (jj % 4) * 128
                        c.do("te", lambda e, jj=jj, bk=bk, col=col, st=st: e.matmul(
                            bk[:, col:col + 128],
                            xin4[:, st * D + jj * 128: st * D + jj * 128 + 128],
                            ident_b[:], start=True, stop=True),
                            inc="pe" if jj == 7 else None)
                    tp_m = c.cnt["pe"]
                    c.w("ve", "pe", tp_m)
                    c.w("ve", "sc", attn_done["sc"])
                    for half in range(2):
                        c.do("ve", lambda e, half=half, st=st: e.tensor_copy(
                            xinT[:].rearrange("p (k w) -> p k w", w=512)
                            [:, half * 4:(half + 1) * 4, st * 128:(st + 1) * 128],
                            B[half][:].rearrange("p (k w) -> p k w", w=128)),
                            inc="ve" if half == 1 else None)
                    xin_m = c.cnt["ve"]
                    mark("bank0", "ve", xin_m)
                    mark("bank1", "ve", xin_m)
                xinT_ready = ("ve", c.cnt["ve"])
                xin4_free = ("pe", c.cnt["pe"])

                pref_pair, pref_next = pref_next, False
                gelu_cnt_start = c.cnt["sc"]
                for fc in range(32):
                    if e_ == 0 and n == 0:
                        drive_c(1, 6)
                    g = fc // 4
                    if fc % 4 == 0 and not (pref_pair and g < 2):
                        slot = g % 2
                        if w1_slot_free[slot] is not None:
                            c.w("sy", *w1_slot_free[slot])
                        c.w("sy", "dWc", wc[e_])
                        c.do("sy", lambda e, e_=e_, g=g, slot=slot: e.dma_start(
                            out=w1g[slot][:].rearrange("p (k w) -> p k w", w=512),
                            in_=d_w1b[e_].rearrange(
                                "p (g k f) -> p g k f", g=8, k=8)[:, g]),
                            inc="dW1a" if slot == 0 else "dW1b", inc_n=16)
                    slot = g % 2
                    wsem = "dW1a" if slot == 0 else "dW1b"
                    bk = B[2 + (fc % 2)]
                    c.w("te", wsem, c.cnt[wsem])
                    c.w("te", *xinT_ready)
                    wait_mark("te", f"bank{2 + (fc % 2)}")
                    for dinc in range(8):
                        c.do("te", lambda e, fc=fc, dinc=dinc, bk=bk, slot=slot: e.matmul(
                            bk[:],
                            w1g[slot][:, dinc * 512 + (fc % 4) * 128:
                                      dinc * 512 + (fc % 4) * 128 + 128],
                            xinT[:, dinc * 512: dinc * 512 + 512],
                            start=(dinc == 0), stop=(dinc == 7)),
                            inc="pe" if dinc == 7 else None)
                    h_m = c.cnt["pe"]
                    if fc % 4 == 3:
                        w1_slot_free[slot] = ("pe", h_m)
                    c.w("sc", "pe", h_m)
                    c.do("sc", lambda e, fc=fc, bk=bk: e.activation(
                        hT[:, fc * 512:(fc + 1) * 512], bk[:], AF.Gelu_apprx_tanh),
                        inc="sc")
                    mark(f"bank{2 + (fc % 2)}", "sc", c.cnt["sc"])
                h_done = c.cnt["sc"]
                xinT_free = ("pe", c.cnt["pe"])
                nxt = (e_, 1) if n == 0 else (e_ + 1, 0)
                if nxt[0] < NE:
                    if nxt != (0, 1):
                        ga_pending = emit_gathers(*nxt)
                    for g2 in (0, 1):
                        if w1_slot_free[g2] is not None:
                            c.w("sy", *w1_slot_free[g2])
                        c.w("sy", "dWc", wc[nxt[0]])
                        c.do("sy", lambda e, ne_=nxt[0], g=g2, slot=g2: e.dma_start(
                            out=w1g[slot][:].rearrange("p (k w) -> p k w", w=512),
                            in_=d_w1b[ne_].rearrange(
                                "p (g k f) -> p g k f", g=8, k=8)[:, g]),
                            inc="dW1a" if g2 == 0 else "dW1b", inc_n=16)
                    pref_next = True
                if not (e_ == 0 and n == 0):
                    nxt_cast = next(cast_iter, None)
                    if nxt_cast is not None:
                        ce, hh = nxt_cast
                        for cg in range(hh * 4, hh * 4 + 4):
                            c.do("gp", lambda e, ce=ce, cg=cg: e.dma_start(
                                out=d_w1b[ce].rearrange(
                                    "p (g k f) -> p g k f", g=8, k=8)[:, cg],
                                in_=dw1[ce].rearrange(
                                    "(k p) (g f) -> p g k f", p=128, f=512)
                                [:, cg]),
                                inc="dWc", inc_n=16)
                        if hh == 1:
                            wc[ce] = c.cnt["dWc"]

                for st in range(4):
                    if e_ == 0 and n == 0:
                        drive_c(1, 6)
                    colx = nst4[st]
                    yb = (B[4], B[5]) if st % 2 == 0 else (B[6], B[7])
                    ybn = ("bank4", "bank5") if st % 2 == 0 else ("bank6", "bank7")
                    c.w("te", "sc", h_done)
                    c.w("te", "dW2", w2_ld)
                    wait_mark("te", ybn[0])
                    wait_mark("te", ybn[1])
                    for fc in range(32):
                        for dh in range(2):
                            c.do("te", lambda e, st=st, dh=dh, fc=fc, yb=yb: e.matmul(
                                yb[dh][:],
                                hT[:, fc * 512 + st * 128: fc * 512 + st * 128 + 128],
                                w2b[:, fc * D + dh * 512: fc * D + dh * 512 + 512],
                                start=(fc == 0), stop=(fc == 31)),
                                inc="pe" if (dh == 1 and fc == 31) else None)
                    y_m = c.cnt["pe"]
                    ywb = y_w[st % 2]
                    c.w("sc", "pe", y_m)
                    wait_mark("sc", f"y_w{st % 2}")
                    for dh in range(2):
                        c.do("sc", lambda e, dh=dh, ywb=ywb, yb=yb, colx=colx: e.activation(
                            ywb[:, dh * 512:(dh + 1) * 512], yb[dh][:], AF.Copy,
                            scale=m_sb[:, 2 * colx + 1: 2 * colx + 2]),
                            inc="sc" if dh == 1 else None)
                    yev_m = c.cnt["sc"]
                    mark(ybn[0], "sc", yev_m)
                    mark(ybn[1], "sc", yev_m)
                    csem = "dScA" if n == 0 else "dScB"
                    c.w("gp", "sc", yev_m)
                    c.w("gp", "dAcc", 16 * (NROW // 128))
                    c.w("gp", csem, scat_chain[n])
                    c.do("gp", lambda e, ywb=ywb, colx=colx: e.indirect_dma_start(
                        out=d_acc,
                        out_offset=IndirectOffsetOnAxis(m_i32[:, colx:colx + 1], 0),
                        in_=ywb[:], in_offset=None,
                        bounds_check=bnd["acc"], oob_is_err=False,
                        compute_op=OP.add),
                        inc=csem, inc_n=16)
                    mark(f"y_w{st % 2}", csem, c.cnt[csem])
                scat_chain[n] = c.cnt["dScA" if n == 0 else "dScB"]
                w2_free = ("pe", c.cnt["pe"])

        if "E" not in _inc:
            phase_gate()
            return c.cnt
        phase_gate()
        x1f_all = c.cnt["dX1"]
        for r in range(NROW // 128):
            pb = r % 2
            col = pb * 2
            wait_mark("sy", f"e_ld{pb}")
            c.w("sy", "dScA", scat_chain[0])
            c.w("sy", "dScB", scat_chain[1])
            c.w("sy", "dX1", x1f_all)
            c.w("sy", "dMt", c.cnt["dMt"])
            c.do("sy", lambda e, r=r, pb=pb: e.dma_start(
                out=e_x1[pb][:], in_=d_x1f[r * 128:(r + 1) * 128, :]), inc="dEx", inc_n=16)
            c.do("sy", lambda e, r=r, pb=pb: e.dma_start(
                out=e_ac[pb][:], in_=d_acc[r * 128:(r + 1) * 128, :]), inc="dEx", inc_n=16)
            ld_m = c.cnt["dEx"]
            c.w("ve", "dEx", ld_m)
            c.w("ve", "pe", c.cnt["pe"])
            wait_mark("ve", f"e_u{pb}")
            c.do("ve", lambda e, pb=pb: e.tensor_copy(e_af[pb][:], e_ac[pb][:]))
            c.do("ve", lambda e, pb=pb: e.tensor_add(e_u[pb][:], e_x1[pb][:], e_af[pb][:]),
                 inc="ve")
            mark(f"e_ld{pb}", "ve", c.cnt["ve"])
            wait_mark("ve", f"e_st{pb}")
            c.do("ve", lambda e, pb=pb, col=col: e.tensor_reduce(
                stats[:, col:col + 1], e_u[pb][:], mybir.AxisListType.X, OP.add),
                flush=True)
            c.do("ve", lambda e, col=col: e.tensor_scalar(
                stats[:, col:col + 1], stats[:, col:col + 1], -1.0 / D, None, OP.mult),
                inc="ve")
            nm = c.cnt["ve"]
            c.w("sc", "ve", nm)
            wait_mark("sc", f"e_af{pb}")
            c.do("sc", lambda e, pb=pb, col=col: e.activation(
                e_af[pb][:], e_u[pb][:], AF.Identity, bias=stats[:, col:col + 1]),
                inc="sc")
            c.do("sc", lambda e, pb=pb, col=col: e.activation(
                e_u[pb][:], e_af[pb][:], AF.Square,
                accum_out=stats[:, col + 1:col + 2]), inc="sc")
            mark(f"e_u{pb}", "sc", c.cnt["sc"])
            c.w("ve", "sc", c.cnt["sc"])
            c.do("ve", lambda e, col=col: e.tensor_scalar(
                stats[:, col + 1:col + 2], stats[:, col + 1:col + 2],
                1.0 / D, 1e-5, OP.mult, OP.add), inc="ve")
            c.w("sc", "ve", c.cnt["ve"])
            c.do("sc", lambda e, col=col: e.activation(
                stats[:, col + 1:col + 2], stats[:, col + 1:col + 2], AF.Ln),
                flush=True)
            c.do("sc", lambda e, col=col: e.activation(
                stats[:, col + 1:col + 2], stats[:, col + 1:col + 2],
                AF.Exp, scale=-0.5), inc="sc")
            c.w("ve", "sc", c.cnt["sc"])
            wait_mark("ve", f"e_o{pb}")
            c.do("ve", lambda e, pb=pb, col=col: e.tensor_scalar(
                e_o[pb][:], e_af[pb][:], stats[:, col + 1:col + 2], None, OP.mult),
                inc="ve")
            o_m = c.cnt["ve"]
            mark(f"e_af{pb}", "ve", o_m)
            mark(f"e_st{pb}", "ve", o_m)
            c.w("sy", "ve", o_m)
            s_ = r // TT
            t0 = (r % TT) * 128
            c.do("sy", lambda e, pb=pb, s_=s_, t0=t0: e.dma_start(
                out=dout[s_, t0:t0 + 128, :], in_=e_o[pb][:]), inc="dOut", inc_n=16)
            mark(f"e_o{pb}", "dOut", c.cnt["dOut"])

        c.w("sy", "dOut", c.cnt["dOut"])
        c.w("gp", "dTap", c.cnt["dTap"])

        return c.cnt

    with nc.Block() as block:
        final_counts = {}

        def emit(eng_name):
            def f(engine):
                ctx = Ctx(nc, sems, engines, eng_name, first=not final_counts)
                cnts = walk(ctx)
                if final_counts:
                    assert cnts == final_counts["c"], f"count drift on {eng_name}"
                else:
                    final_counts["c"] = cnts
            return f

        block.sync(emit("sy"))
        block.tensor(emit("te"))
        block.scalar(emit("sc"))
        block.vector(emit("ve"))
        block.gpsimd(emit("gp"))

    return nc


_NC_CACHE = {}


def _get_nc():
    if "nc" not in _NC_CACHE:
        _NC_CACHE["nc"] = build_nc()
    return _NC_CACHE["nc"]


def kernel(L0_attn_inputs, L0_attn_Wq, L0_attn_bq, L0_attn_Wk, L0_attn_bk,
           L0_attn_Wv, L0_attn_bv, L0_attn_Wo, L0_attn_bo, L0_attn_gamma,
           L0_attn_beta, L0_ffn_gate_w, L0_ffn_w1, L0_ffn_w2):
    x = np.ascontiguousarray(np.asarray(L0_attn_inputs, dtype=np.float32))
    ne4 = np.zeros((40, 1), np.float32)
    for p in range(8):
        ne4[p, 0] = p * 4
        ne4[32 + p, 0] = (8 + p) * 4
    common = {
        "Wq": np.ascontiguousarray(L0_attn_Wq, dtype=np.float32),
        "Wk": np.ascontiguousarray(L0_attn_Wk, dtype=np.float32),
        "Wv": np.ascontiguousarray(L0_attn_Wv, dtype=np.float32),
        "Wo": np.ascontiguousarray(L0_attn_Wo, dtype=np.float32),
        "bqT": np.ascontiguousarray(np.asarray(L0_attn_bq, np.float32).reshape(8, 128).T),
        "bkT": np.ascontiguousarray(np.asarray(L0_attn_bk, np.float32).reshape(8, 128).T),
        "bv": np.ascontiguousarray(np.asarray(L0_attn_bv, np.float32).reshape(1, D)),
        "bo": np.ascontiguousarray(np.asarray(L0_attn_bo, np.float32).reshape(1, D)),
        "gamma": np.ascontiguousarray(np.asarray(L0_attn_gamma, np.float32).reshape(1, D)),
        "beta": np.ascontiguousarray(np.asarray(L0_attn_beta, np.float32).reshape(1, D)),
        "gate_w": np.ascontiguousarray(L0_ffn_gate_w, dtype=np.float32),
        "w1": np.ascontiguousarray(L0_ffn_w1, dtype=np.float32),
        "w2": np.ascontiguousarray(L0_ffn_w2, dtype=np.float32),
        "ident": np.eye(128, dtype=np.float32),
        "ne4": ne4,
    }
    in_maps = []
    for core in range(8):
        m = dict(common)
        m["x"] = np.ascontiguousarray(x[2 * core:2 * core + 2])
        m["xt"] = np.ascontiguousarray(m["x"].transpose(0, 2, 1))
        in_maps.append(m)
    nc = _get_nc()
    res = run_bass_kernel_spmd(nc, in_maps, core_ids=list(range(8)))
    out = np.concatenate([r["out"] for r in res.results], axis=0)
    return out.reshape(16, S, D).astype(np.float32)
